# version 11
# baseline (speedup 1.0000x reference)
import numpy as np
import concourse.bass as bass
import concourse.mybir as mybir
from concourse.bass_utils import run_bass_kernel_spmd
from contextlib import ExitStack

F32 = mybir.dt.float32
BF16 = mybir.dt.bfloat16
AF = mybir.ActivationFunctionType
ALU = mybir.AluOpType

D, T = 2048, 2048
NCH, NH, NI = 16, 16, 44
TG, NG = 512, 4
CH, NCHK = 64, 32
EPS = 1e-5
DMA_SEMS = ("dxa", "dcl0", "dcl1", "dcs0", "dcs1", "dsp", "dhs0", "dhs1",
            "dos", "dgx0", "dgx1", "dqk0", "dqk1", "dxl", "dgu0", "dgu1",
            "dhl0", "dhl1", "ddl0", "ddl1", "dwq", "doc", "dw30", "dw31",
            "dg30", "dg31", "dwg0", "dwg1", "dcc", "dout", "dbgs")

DEBUG_TAP = 0
DEBUG_DUMP = 0


class Buf:
    __slots__ = ("wr", "rd")

    def __init__(self):
        self.wr = None
        self.rd = []


class Ctx:
    def __init__(self, nc, block, sems):
        self.nc = nc
        self.block = block
        self.sems = sems
        self.cnt = {k: 0 for k in sems}
        self.wm = {}
        self.gate = {}
        self.named = {}

    def close_scope(self, bufs):
        for b in bufs:
            for ev in ([b.wr] if b.wr else []) + b.rd:
                k, c = ev
                if self.gate.get(k, -1) < c:
                    self.gate[k] = c


class E:
    def __init__(self, ctx, e, eng):
        self.ctx = ctx
        self.e = e
        self.eng = eng

    def _wait(self, kind, c):
        key = (self.eng, kind)
        if self.ctx.wm.get(key, -1) >= c:
            return
        self.e.wait_ge(self.ctx.sems[kind], c)
        self.ctx.wm[key] = c

    def waits(self, outs=(), ins=(), also=()):
        for k, c in self.ctx.gate.items():
            self._wait(k, c)
        for b in outs:
            if b.wr:
                self._wait(*b.wr)
            for r in b.rd:
                self._wait(*r)
        for b in ins:
            if b.wr:
                self._wait(*b.wr)
        for ev in also:
            if ev:
                self._wait(*ev)

    def op(self, kind, fn, outs=(), ins=(), also=(), cc=False):
        self.waits(outs, ins, also)
        inst = fn(self.e)
        insts = inst if isinstance(inst, (list, tuple)) else [inst]
        for it in insts:
            if cc:
                self.ctx.cnt[kind] += 1
                it.then_inc(self.ctx.sems[kind])
            else:
                inc = 16 if kind.startswith("d") else 1
                self.ctx.cnt[kind] += inc
                it.then_inc(self.ctx.sems[kind], inc)
        ev = (kind, self.ctx.cnt[kind])
        for b in outs:
            b.wr = ev
            b.rd = []
        for b in ins:
            b.rd.append(ev)
        return ev


def build_nc():
    nc = bass.Bass("TRN2", target_bir_lowering=False)

    def ein(name, shape):
        return nc.dram_tensor(name, shape, F32, kind="ExternalInput").ap()

    xT = ein("xT", [D, T])
    dre_i = ein("dre", [128, 64])
    dim_i = ein("dim", [128, 64])
    ire_i = ein("ire", [64, 128])
    iim_i = ein("iim", [64, 128])
    ident_i = ein("ident", [128, 128])
    triu_i = ein("triu64", [64, 64])
    onesr_i = ein("onesr", [1, 128])
    onesc_i = ein("onesc", [128, 1])
    pmask_i = ein("pmask", [128, 1])
    cbias_i = ein("cbias", [128, 2])
    aWq = ein("aWq", [NH, 128, D])
    aWkv = ein("aWkv", [NH, 128, D])
    aWo = ein("aWo", [NCH, 128, D])
    gw = ein("gw", [NI, 128, D])
    uw = ein("uw", [NI, 128, D])
    dw = ein("dw", [NI, 128, D])
    bWq = ein("bWq", [128, D])
    bWk = ein("bWk", [128, D])
    wog = ein("wog", [NCH, 128, D])
    wig = ein("wig", [NCH, 128, D])
    wob = ein("wob", [NCH, 128, D])
    gnw = ein("gnw", [128, NCH])
    gsv = ein("gsv", [128, NI])
    usv = ein("usv", [128, NI])
    dsv = ein("dsv", [128, NCH])

    y = nc.dram_tensor("y", [D, T], F32, kind="ExternalOutput").ap()
    dbg = (nc.dram_tensor("dbg", [128, 6144], F32, kind="ExternalOutput").ap()
           if DEBUG_DUMP else None)
    tap = (nc.dram_tensor("tap", [D, T], F32, kind="ExternalOutput").ap()
           if DEBUG_TAP else None)

    scrd = lambda n, s: nc.dram_tensor(n, s, BF16).ap()
    aWq_b = scrd("aWq_b", [NH, 128, D])
    aWkv_b = scrd("aWkv_b", [NH, 128, D])
    aWo_b = scrd("aWo_b", [NCH, 128, D])
    gw_b = scrd("gw_b", [NI, 128, D])
    uw_b = scrd("uw_b", [NI, 128, D])
    dw_b = scrd("dw_b", [NI, 128, D])
    bWq_b = scrd("bWq_b", [128, D])
    bWk_b = scrd("bWk_b", [128, D])
    wog_b = scrd("wog_b", [NCH, 128, D])
    wig_b = scrd("wig_b", [NCH, 128, D])
    wob_b = scrd("wob_b", [NCH, 128, D])
    x1_scr = scrd("x1_scr", [NG, 128, NCH * TG])
    x3_scr = scrd("x3_scr", [NG, 128, NCH * TG])
    h_scr = scrd("h_scr", [NI, 128, 1024])
    o_scr = scrd("o_scr", [NCHK, 64, D])
    gox_scr = scrd("gox_scr", [NG, 128, NCH * TG])
    warm_scr = scrd("warm_scr", [128, 1024])
    cca_in = nc.dram_tensor("cca_in", [128, 16], F32).ap()
    cca_out = nc.dram_tensor("cca_out", [256, 16], F32).ap()
    ccs_in = nc.dram_tensor("ccs_in", [128, D], F32).ap()
    ccs_out = nc.dram_tensor("ccs_out", [256, D], F32).ap()

    es = ExitStack()
    sems = {k: es.enter_context(nc.semaphore(k)) for k in
            (("pe", "act", "dve", "gp", "cc") + DMA_SEMS)}

    BUFS = {}
    _ncnt = [0]

    def un(base):
        _ncnt[0] += 1
        return f"{base}_{_ncnt[0]}"

    def sb(name, shape, dtype, **kw):
        t = es.enter_context(nc.sbuf_tensor(name, shape, dtype, **kw))
        BUFS[name] = Buf()
        return t

    pfull = es.enter_context(nc.psum_tensor("pfull", [128, 4096], F32))
    PB = [Buf() for _ in range(8)]

    def pk(bank, p0=0, pl=128, c0=0, cl=512):
        return pfull[p0:p0 + pl, bank * 512 + c0:bank * 512 + c0 + cl]

    def pkb(bank, p0=0, pl=128, c0=0, cl=1024):
        return pfull[:, bank * 512:(bank + 1) * 512].bitcast(BF16)[
            p0:p0 + pl, c0:c0 + cl]

    x_res = sb("x_res", [128, NCH * T], F32)
    c_dre = sb("c_dre", [128, 64], F32)
    c_dim = sb("c_dim", [128, 64], F32)
    c_onescf = sb("c_onescf", [128, 1], F32)
    c_dre_b = sb("c_dre_b", [128, 64], BF16)
    c_dim_b = sb("c_dim_b", [128, 64], BF16)
    c_ire_b = sb("c_ire_b", [64, 128], BF16)
    c_iim_b = sb("c_iim_b", [64, 128], BF16)
    c_onesr_b = sb("c_onesr_b", [1, 128], BF16)
    c_idb = sb("c_idb", [128, 128], BF16)
    c_idf = sb("c_idf", [1, 1], F32)
    c_triu = sb("c_triu", [64, 64], F32)
    c_onesr = sb("c_onesr", [1, 128], F32)
    c_onesc = sb("c_onesc", [128, 1], BF16)
    c_pmask = sb("c_pmask", [128, 1], F32)
    c_cb = sb("c_cb", [128, 2], F32)
    rows = sb("rows", [1, 1024], F32)
    rows2 = sb("rows2", [1, 1024], F32)
    rowsb = sb("rowsb", [1, 2048], BF16)
    carryA = sb("carryA", [128, 16], F32)
    carryBt = sb("carryBt", [128, 16], BF16)
    scanc = sb("scanc", [64, 32], F32)
    pcF = sb("pcF", [64, 32], F32)
    vecs = sb("vecs", [128, NCH + 2 * NI + NCH], F32)
    GNW0, GS0, US0, DS0 = 0, NCH, NCH + NI, NCH + 2 * NI
    _stW = ExitStack()
    wstg = [_stW.enter_context(nc.sbuf_tensor(
        f"wstg{i}", [128, 512], BF16, side="right")) for i in range(2)]
    wstgf = [_stW.enter_context(nc.sbuf_tensor(
        f"wstgf{i}", [128, 512], F32, side="right")) for i in range(2)]

    B = lambda n: BUFS[n]
    CONSTS = Buf()
    ROWS = B("rows")
    ROWS2 = B("rows2")
    XRESG = [Buf() for _ in range(NG)]
    VECS = B("vecs")

    with nc.Block() as block:
        ctx = Ctx(nc, block, sems)

        def sec(eng, fn):
            getattr(block, eng)(lambda e: fn(E(ctx, e, eng)))

        def _i1(E_):
            for c in range(NCH):
                E_.op("dxa", lambda e, c=c: e.dma_start(
                    out=x_res[:, c * T:(c + 1) * T],
                    in_=xT[c * 128:(c + 1) * 128, :]), outs=XRESG)
            for dst, src in ((c_triu, triu_i),
                             (c_onesr, onesr_i), (c_pmask, pmask_i),
                             (c_cb, cbias_i)):
                pass
            E_.op("dxa", lambda e: e.dma_start(
                out=c_idf[:], in_=ident_i[0:1, 0:1]), outs=[CONSTS])
            for dst, src in ((c_triu, triu_i),
                             (c_onesr, onesr_i), (c_pmask, pmask_i),
                             (c_cb, cbias_i)):
                E_.op("dxa", lambda e, d=dst, s=src: e.dma_start(
                    out=d[:], in_=s[:, :]), outs=[CONSTS])
            for off, src, n in ((GNW0, gnw, NCH), (GS0, gsv, NI),
                                (US0, usv, NI), (DS0, dsv, NCH)):
                E_.op("dxa", lambda e, o=off, s=src, n=n: e.dma_start(
                    out=vecs[:, o:o + n], in_=s[:, :]), outs=[VECS])
        sec("sync", _i1)

        def _i2(E_):
            for dst, src in ((c_idb, ident_i), (c_onesc, onesc_i),
                             (c_dre_b, dre_i), (c_dim_b, dim_i),
                             (c_ire_b, ire_i), (c_iim_b, iim_i),
                             (c_onesr_b, onesr_i)):
                E_.op("dxa", lambda e, d=dst, s=src: e.dma_start(
                    out=d[:], in_=s[:, :]), outs=[CONSTS])
        sec("gpsimd", _i2)

        def _i2b(E_):
            for dst, src in ((c_dre, dre_i), (c_dim, dim_i),
                             (c_onescf, onesc_i)):
                E_.op("dxa", lambda e, d=dst, s=src: e.dma_start(
                    out=d[:], in_=s[:, :]), outs=[CONSTS])
        sec("sync", _i2b)

        CARA, CARB = B("carryA"), B("carryBt")
        SCANC, PCF = B("scanc"), B("pcF")

        def _i3(E_):
            E_.op("dve", lambda e: e.memset(carryA[:], 0.0), outs=[CARA])
            E_.op("dve", lambda e: e.memset(scanc[:], 0.0), outs=[SCANC])
            E_.op("dve", lambda e: e.memset(pcF[:], 0.0), outs=[PCF])
        sec("vector", _i3)

        cast_specs = [("aWkv", aWkv, aWkv_b, NH), ("aWq", aWq, aWq_b, NH),
                      ("aWo", aWo, aWo_b, NCH), ("gw", gw, gw_b, NI),
                      ("uw", uw, uw_b, NI), ("dw", dw, dw_b, NI),
                      ("wig", wig, wig_b, NCH), ("bWq", bWq, bWq_b, 0),
                      ("bWk", bWk, bWk_b, 0), ("wog", wog, wog_b, NCH),
                      ("wob", wob, wob_b, NCH)]
        jobs = [("warm", aWkv[0, :, q * 512:(q + 1) * 512],
                 warm_scr[:, 0:512], False) for q in range(4)]
        for nm, src, dst, n in cast_specs:
            if n == 0:
                for q in range(4):
                    jobs.append((nm, src[:, q * 512:(q + 1) * 512],
                                 dst[:, q * 512:(q + 1) * 512], q == 3))
            else:
                for i in range(n):
                    for q in range(4):
                        jobs.append((nm,
                                     src[i, :, q * 512:(q + 1) * 512],
                                     dst[i, :, q * 512:(q + 1) * 512],
                                     i == n - 1 and q == 3))
        act_c0 = [ctx.cnt["act"]]

        dxa_init = ctx.cnt["dxa"]
        cast_copy_ev = {}
        cast_pos = [0]

        def emit_cast(njobs):
            j0 = cast_pos[0]
            j1 = min(j0 + njobs, len(jobs))
            cast_pos[0] = j1
            if j1 <= j0:
                return

            act_base = ctx.cnt["act"]

            def _cl(E_):
                if j0 == 0:
                    E_.waits(also=[("dxa", dxa_init)])
                for j in range(j0, j1):
                    nm, s, d, last = jobs[j]
                    p = j % 2
                    if j - 2 >= j0:
                        E_.waits(also=[("act", act_base + (j - 2 - j0) + 1)])
                    elif j >= 2:
                        E_.waits(also=[cast_copy_ev[j - 2]])
                    E_.op(f"dcl{p}", lambda e, s=s: e.dma_start(
                        out=wstgf[p][:], in_=s))
            sec("sync", _cl)

            def _cs(E_):
                assert ctx.cnt["act"] == act_base
                for j in range(j0, j1):
                    nm, s, d, last = jobs[j]
                    p = j % 2
                    E_.waits(also=[(f"dcl{p}", 16 * (j // 2 + 1))])
                    if j >= 2:
                        E_.waits(also=[(f"dcs{p}", 16 * (j // 2))])
                    ev = E_.op("act", lambda e, p=p: e.copy(wstg[p][:],
                                                            wstgf[p][:]))
                    cast_copy_ev[j] = ev
                    ev2 = E_.op(f"dcs{p}", lambda e, d=d, p=p: e.dma_start(
                        out=d, in_=wstg[p][:]))
                    if last:
                        ctx.named["cast_" + nm] = ev2
            sec("scalar", _cs)

        def finish_casts():
            emit_cast(len(jobs))
            for k in ("dcs0", "dcs1"):
                ctx.gate[k] = max(ctx.gate.get(k, 0), ctx.cnt[k])
            _stW.close()

        emit_cast(len(jobs))

        def cast_ev(nm):
            return ctx.named["cast_" + nm]

        LN_MR = {}

        def emit_ln(tok0, dst_view, lnb, lns, lntmp, LNB, LNS, LNT,
                    bank_s, bank_q, bank_m, bank_r, mrow, rrow,
                    dst_buf=None, mr_key=0):
            XRES = XRESG[tok0 // TG]
            MR = LN_MR.setdefault("all", Buf())
            for half in range(4):
                c0, c1 = half * 4, half * 4 + 4

                def _dv(E_, c0=c0, c1=c1):
                    for c in range(c0, c1):
                        E_.op("dve", lambda e, c=c: e.tensor_copy(
                            lnb[:, (c % 4) * TG:(c % 4 + 1) * TG],
                            x_res[:, c * T + tok0:c * T + tok0 + TG]),
                            outs=[LNB], ins=[XRES])
                sec("vector", _dv)

                def _ac(E_, c0=c0, c1=c1):
                    for c in range(c0, c1):
                        E_.op("act", lambda e, c=c: e.activation(
                            lns[:, (c % 4) * TG:(c % 4 + 1) * TG],
                            x_res[:, c * T + tok0:c * T + tok0 + TG],
                            AF.Square), outs=[LNS], ins=[XRES])
                sec("scalar", _ac)

                def _pe(E_, half=half):
                    def f(e):
                        for c in range(4):
                            i = e.matmul(pk(bank_s, 0, 1), c_onesc[:, :],
                                         lnb[:, c * TG:(c + 1) * TG],
                                         start=(half == 0 and c == 0),
                                         stop=(half == 3 and c == 3))
                        return i
                    E_.op("pe", f, outs=[PB[bank_s]], ins=[LNB, CONSTS])

                    def f2(e):
                        for c in range(4):
                            i = e.matmul(pk(bank_q, 0, 1), c_onesc[:, :],
                                         lns[:, c * TG:(c + 1) * TG],
                                         start=(half == 0 and c == 0),
                                         stop=(half == 3 and c == 3))
                        return i
                    E_.op("pe", f2, outs=[PB[bank_q]], ins=[LNS, CONSTS])
                sec("tensor", _pe)

            def _a2(E_):
                E_.op("act", lambda e: e.activation(
                    mrow, pk(bank_s, 0, 1), AF.Copy, scale=1.0 / D),
                    outs=[MR], ins=[PB[bank_s]])
                E_.op("act", lambda e: e.activation(
                    rrow, pk(bank_s, 0, 1), AF.Square, scale=1.0 / D),
                    outs=[MR], ins=[PB[bank_s]])
            sec("scalar", _a2)

            def _d2(E_):
                E_.op("dve", lambda e: e.scalar_tensor_tensor(
                    rrow, pk(bank_q, 0, 1), 1.0 / D, rrow,
                    ALU.mult, ALU.subtract), outs=[MR], ins=[PB[bank_q], MR])
            sec("vector", _d2)

            def _a3(E_):
                E_.op("act", lambda e: e.activation(rrow, rrow, AF.Sqrt,
                                                    bias=c_cb[0:1, 0:1]),
                      outs=[MR], ins=[MR, CONSTS])
            sec("scalar", _a3)

            def _d3(E_):
                E_.op("dve", lambda e: e.reciprocal(rrow, rrow),
                      outs=[MR], ins=[MR])
            sec("vector", _d3)

            def _p2(E_):
                E_.op("pe", lambda e: e.matmul(pk(bank_m), c_onesr[:, :],
                                               mrow, start=True, stop=True),
                      outs=[PB[bank_m]], ins=[MR, CONSTS])
                E_.op("pe", lambda e: e.matmul(pk(bank_r), c_onesr[:, :],
                                               rrow, start=True, stop=True),
                      outs=[PB[bank_r]], ins=[MR, CONSTS])
            sec("tensor", _p2)

            DSTB = dst_buf if dst_buf is not None else Buf()

            def _d4(E_):
                for c in range(NCH):
                    E_.op("dve", lambda e, c=c: e.tensor_tensor(
                        lntmp[:, :],
                        x_res[:, c * T + tok0:c * T + tok0 + TG],
                        pk(bank_m), ALU.subtract),
                        outs=[LNT], ins=[XRES, PB[bank_m]])
                    E_.op("dve", lambda e, c=c: e.tensor_tensor(
                        dst_view[:, c, :], lntmp[:, :], pk(bank_r), ALU.mult),
                        outs=[DSTB], ins=[LNT, PB[bank_r]])
            sec("vector", _d4)
            return DSTB

        _stA = ExitStack()
        _cA = lambda nm, sh, dt: _stA.enter_context(nc.sbuf_tensor(nm, sh, dt))
        xlnA = _cA("xlnA", [128, NCH * TG], BF16)
        kvsl = [_cA(f"kvsl{i}", [128, D], BF16) for i in range(2)]
        kvbf = _cA("kvbf", [128, TG], BF16)
        sqh = _cA("sqh", [128, TG], BF16)
        kvn = _cA("kvn", [128, TG], BF16)
        accA = _cA("accA", [128, 1], F32)
        if True:
            XLN, KVSL0, KVSL1 = Buf(), Buf(), Buf()
            KVBF, SQH, KVN, ACC = Buf(), Buf(), Buf(), Buf()
            KVSL = [KVSL0, KVSL1]
            mrow, rrow = rows[0:1, 0:TG], rows[0:1, TG:2 * TG]
            rkrow = rows2[0:1, 0:TG]
            RK = Buf()
            for g in range(NG):
                with (
                    nc.sbuf_tensor(un("lnbA"), [128, 4 * TG], BF16) as lnb,
                    nc.sbuf_tensor(un("lnsA"), [128, 4 * TG], BF16) as lns,
                    nc.sbuf_tensor(un("lntA"), [128, TG], F32) as lnt,
                ):
                    LNB, LNS, LNT = Buf(), Buf(), Buf()
                    emit_ln(g * TG,
                            xlnA[:].rearrange("p (c t) -> p c t", c=NCH),
                            lnb, lns, lnt, LNB, LNS, LNT,
                            0, 1, 2, 3, mrow, rrow, dst_buf=XLN)
                    ctx.close_scope([LNB, LNS, LNT])

                def _sp(E_, g=g):
                    ev = E_.op("dsp", lambda e: e.dma_start(
                        out=x1_scr[g, :, :], in_=xlnA[:]), ins=[XLN])
                    ctx.named[f"x1sp{g}"] = ev
                sec("scalar", _sp)

                for h in range(NH):
                    par = h % 2
                    bk = 4 + par

                    def _ld(E_, h=h, par=par):
                        E_.op(f"dqk{par}", lambda e: e.dma_start(
                            out=kvsl[par][:, :], in_=aWkv_b[h, :, :]),
                            outs=[KVSL[par]], also=[cast_ev("aWkv")])
                    sec("sync", _ld)
                    emit_cast(4)

                    def _pe(E_, h=h, par=par, bk=bk):
                        def f(e):
                            for c in range(NCH):
                                i = e.matmul(pk(bk),
                                             kvsl[par][:, c * 128:(c + 1) * 128],
                                             xlnA[:, c * TG:(c + 1) * TG],
                                             start=(c == 0),
                                             stop=(c == NCH - 1))
                            return i
                        E_.op("pe", f, outs=[PB[bk]], ins=[KVSL[par], XLN])
                    sec("tensor", _pe)

                    def _ac(E_, bk=bk):
                        E_.op("act", lambda e: e.copy(kvbf[:], pk(bk)),
                              outs=[KVBF], ins=[PB[bk]])
                        E_.op("act", lambda e: e.activation(
                            sqh[:], kvbf[:], AF.Square),
                            outs=[SQH], ins=[KVBF])
                    sec("scalar", _ac)

                    def _p2(E_):
                        E_.op("pe", lambda e: e.matmul(
                            pk(6, 0, 1), c_onesc[:, :], sqh[:],
                            start=True, stop=True),
                            outs=[PB[6]], ins=[SQH, CONSTS])
                    sec("tensor", _p2)

                    def _a2(E_):
                        E_.op("act", lambda e: e.activation(
                            rkrow, pk(6, 0, 1), AF.Sqrt, scale=128.0),
                            outs=[RK], ins=[PB[6]])
                    sec("scalar", _a2)

                    def _d1(E_):
                        E_.op("dve", lambda e: e.reciprocal(rkrow, rkrow),
                              outs=[RK], ins=[RK])
                    sec("vector", _d1)

                    def _a2b(E_):
                        E_.op("act", lambda e: e.copy(
                            rowsb[0:1, 1024:1024 + TG], rkrow),
                            outs=[RK], ins=[RK])
                    sec("scalar", _a2b)

                    def _p3(E_):
                        E_.op("pe", lambda e: e.matmul(
                            pk(7), c_onesr_b[:, :],
                            rowsb[0:1, 1024:1024 + TG],
                            start=True, stop=True),
                            outs=[PB[7]], ins=[RK, CONSTS])
                    sec("tensor", _p3)

                    def _d2(E_, g=g, h=h):
                        E_.op("dve", lambda e: e.scalar_tensor_tensor(
                            kvn[:], kvbf[:], 1.0, pk(7),
                            ALU.mult, ALU.mult, accum_out=accA[:, :]),
                            outs=[KVN, ACC], ins=[KVBF, PB[7]])
                        E_.op("dve", lambda e: e.tensor_tensor(
                            carryA[:, h:h + 1], carryA[:, h:h + 1],
                            accA[:, :], ALU.add),
                            outs=[CARA], ins=[ACC, CARA])
                    sec("vector", _d2)
            ctx.close_scope([XLN, KVSL0, KVSL1, KVBF, SQH, KVN, ACC, RK])
        _stA.close()

        def _cc(E_):
            ev1 = E_.op("dcc", lambda e: e.dma_start(
                out=cca_in[:, :], in_=carryA[:]), ins=[CARA])
            ev2 = E_.op("cc", lambda e: e.collective_compute(
                "AllGather", ALU.bypass,
                replica_groups=[[0, 1], [2, 3], [4, 5], [6, 7]],
                ins=[cca_in[:, :].opt()], outs=[cca_out[:, :].opt()]),
                cc=True, also=[ev1])
            E_.op("dcc", lambda e: e.dma_start(out=carryA[:],
                                                 in_=cca_out[0:128, :]),
                  outs=[CARA], also=[ev2])
        sec("gpsimd", _cc)

        def _ccm(E_):
            E_.op("dve", lambda e: e.tensor_scalar(
                carryA[:], carryA[:], c_pmask[:, :], None, ALU.mult),
                outs=[CARA], ins=[CARA, CONSTS])
        sec("vector", _ccm)

        def _ccp(E_):
            E_.op("pe", lambda e: e.matmul(pk(0, 0, 64, 0, 16), c_dre[:, :],
                                           carryA[:], start=True, stop=True),
                  outs=[PB[0]], ins=[CARA, CONSTS])
            E_.op("pe", lambda e: e.matmul(pk(0, 0, 64, 16, 16), c_dim[:, :],
                                           carryA[:], start=True, stop=True),
                  outs=[PB[0]], ins=[CARA, CONSTS])
        sec("tensor", _ccp)

        def _cce(E_):
            E_.op("act", lambda e: e.copy(pcF[:], pk(0, 0, 64, 0, 32)),
                  outs=[PCF], ins=[PB[0]])
        sec("scalar", _cce)

        _stB = ExitStack()
        _cB = lambda nm, sh, dt: _stB.enter_context(nc.sbuf_tensor(nm, sh, dt))
        xlnB = _cB("xlnB", [128, NCH * TG], BF16)
        vals = _cB("vals", [128, NH * TG], BF16)
        qksl = [_cB(f"qksl{i}", [128, 2 * D], BF16) for i in range(2)]
        qhB = _cB("qhB", [128, TG], BF16)
        kvhB = _cB("kvhB", [128, TG], BF16)
        sqhB = _cB("sqhB", [128, TG], BF16)
        rqbB = _cB("rqbB", [128, TG], BF16)
        qfB = _cB("qfB", [64, 2 * TG], BF16)
        kvfB = _cB("kvfB", [64, 2 * TG], BF16)
        kvcB = _cB("kvcB", [64, 2 * TG], BF16)
        ttB = sqhB[0:64, 0:TG]
        qvB = kvfB
        if True:
            XLNB, VALS = Buf(), Buf()
            QKSL = [Buf(), Buf()]
            QH, KVH, SQB, RQB = Buf(), Buf(), Buf(), Buf()
            QF, KVF, KVC = Buf(), Buf(), Buf()
            QV = KVF
            rq_row = rows2[0:1, 0:TG]
            rk_row = rows2[0:1, TG:2 * TG]
            rqb_row = rowsb[0:1, 1024:1024 + TG]
            rkb_row = rowsb[0:1, 1536:1536 + TG]
            RQR, RKR = Buf(), Buf()
            for g in range(NG):
                def _lx(E_, g=g):
                    E_.op("dxl", lambda e: e.dma_start(
                        out=xlnB[:], in_=x1_scr[g, :, :]),
                        outs=[XLNB], also=[ctx.named[f"x1sp{g}"]])
                sec("sync", _lx)

                for h in range(NH):
                    par = h % 2
                    bq = 4 + par

                    def _ld(E_, h=h, par=par):
                        def f(e):
                            i1 = e.dma_start(out=qksl[par][:, 0:D],
                                             in_=aWq_b[h, :, :])
                            i2 = e.dma_start(out=qksl[par][:, D:2 * D],
                                             in_=aWkv_b[h, :, :])
                            return [i1, i2]
                        E_.op(f"dqk{par}", f, outs=[QKSL[par]],
                              also=[cast_ev("aWq")])
                    sec("sync", _ld)
                    emit_cast(12)

                    def _pe(E_, h=h, par=par, bq=bq):
                        def f(e):
                            for c in range(NCH):
                                i = e.matmul(pk(bq),
                                             qksl[par][:, c * 128:(c + 1) * 128],
                                             xlnB[:, c * TG:(c + 1) * TG],
                                             start=(c == 0),
                                             stop=(c == NCH - 1))
                            return i
                        E_.op("pe", f, outs=[PB[bq]], ins=[QKSL[par], XLNB])

                        def f2(e):
                            for c in range(NCH):
                                i = e.matmul(
                                    pk(6),
                                    qksl[par][:, D + c * 128:D + (c + 1) * 128],
                                    xlnB[:, c * TG:(c + 1) * TG],
                                    start=(c == 0), stop=(c == NCH - 1))
                            return i
                        E_.op("pe", f2, outs=[PB[6]], ins=[QKSL[par], XLNB])
                    sec("tensor", _pe)

                    def _ac(E_, bq=bq):
                        E_.op("act", lambda e: e.copy(qhB[:], pk(bq)),
                              outs=[QH], ins=[PB[bq]])
                        E_.op("act", lambda e: e.copy(kvhB[:], pk(6)),
                              outs=[KVH], ins=[PB[6]])
                        E_.op("act", lambda e: e.activation(
                            sqhB[:], qhB[:], AF.Square), outs=[SQB], ins=[QH])
                    sec("scalar", _ac)

                    if DEBUG_DUMP and g == 0 and h == 0:
                        def _dbg1(E_):
                            E_.op("dbgs", lambda e: e.dma_start(
                                out=dbg[:, 0:TG], in_=qhB[:]), ins=[QH])
                            E_.op("dbgs", lambda e: e.dma_start(
                                out=dbg[:, 512:1024], in_=kvhB[:]),
                                ins=[KVH])
                        sec("gpsimd", _dbg1)

                    def _p2(E_):
                        E_.op("pe", lambda e: e.matmul(
                            pk(7, 0, 1), c_onesc[:, :], sqhB[:],
                            start=True, stop=True),
                            outs=[PB[7]], ins=[SQB, CONSTS])
                    sec("tensor", _p2)

                    def _a2(E_):
                        E_.op("act", lambda e: e.activation(
                            rq_row, pk(7, 0, 1), AF.Sqrt, scale=128.0),
                            outs=[RQR], ins=[PB[7]])
                        E_.op("act", lambda e: e.activation(
                            sqhB[:], kvhB[:], AF.Square),
                            outs=[SQB], ins=[KVH])
                    sec("scalar", _a2)

                    def _p3(E_):
                        E_.op("pe", lambda e: e.matmul(
                            pk(7, 0, 1), c_onesc[:, :], sqhB[:],
                            start=True, stop=True),
                            outs=[PB[7]], ins=[SQB, CONSTS])
                    sec("tensor", _p3)

                    def _a3(E_):
                        E_.op("act", lambda e: e.activation(
                            rk_row, pk(7, 0, 1), AF.Sqrt, scale=128.0),
                            outs=[RKR], ins=[PB[7]])
                    sec("scalar", _a3)

                    def _d1(E_):
                        E_.op("dve", lambda e: e.reciprocal(rq_row, rq_row),
                              outs=[RQR], ins=[RQR])
                        E_.op("dve", lambda e: e.reciprocal(rk_row, rk_row),
                              outs=[RKR], ins=[RKR])
                    sec("vector", _d1)

                    def _a3b(E_):
                        E_.op("act", lambda e: e.copy(rqb_row, rq_row),
                              outs=[RQR], ins=[RQR])
                        E_.op("act", lambda e: e.copy(rkb_row, rk_row),
                              outs=[RKR], ins=[RKR])
                    sec("scalar", _a3b)

                    def _p4(E_):
                        E_.op("pe", lambda e: e.matmul(
                            pk(7), c_onesr_b[:, :], rkb_row,
                            start=True, stop=True),
                            outs=[PB[7]], ins=[RKR, CONSTS])
                    sec("tensor", _p4)

                    def _d2(E_):
                        E_.op("dve", lambda e: e.tensor_tensor(
                            sqhB[:], kvhB[:], pk(7), ALU.mult),
                            outs=[SQB], ins=[KVH, PB[7]])
                    sec("vector", _d2)

                    def _p5(E_):
                        E_.op("pe", lambda e: e.matmul(
                            pk(7), c_onesr_b[:, :], rqb_row,
                            start=True, stop=True),
                            outs=[PB[7]], ins=[RQR, CONSTS])
                    sec("tensor", _p5)

                    def _a4(E_):
                        E_.op("act", lambda e: e.copy(rqbB[:], pk(7)),
                              outs=[RQB], ins=[PB[7]])
                    sec("scalar", _a4)

                    if DEBUG_DUMP and g == 0 and h == 0:
                        def _dbg2(E_):
                            E_.op("dbgs", lambda e: e.dma_start(
                                out=dbg[:, 1024:1536],
                                in_=kvhB[:]), ins=[KVH])
                            E_.op("dbgs", lambda e: e.dma_start(
                                out=dbg[:, 1536:2048],
                                in_=qksl[0][:, D:D + 512]), ins=[QKSL[0]])
                        sec("gpsimd", _dbg2)

                    def _p6(E_):
                        def f(e):
                            e.matmul(pk(0, 0, 64), c_dre_b[:, :], qhB[:],
                                     start=True, stop=True)
                            e.matmul(pk(1, 0, 64), c_dim_b[:, :], qhB[:],
                                     start=True, stop=True)
                            e.matmul(pk(2, 0, 64), c_dre_b[:, :], sqhB[:],
                                     start=True, stop=True)
                            return e.matmul(pk(3, 0, 64), c_dim_b[:, :],
                                            sqhB[:], start=True, stop=True)
                        E_.op("pe", f, outs=[PB[0], PB[1], PB[2], PB[3]],
                              ins=[QH, SQB, CONSTS])
                    sec("tensor", _p6)

                    def _a5(E_):
                        def f(e):
                            e.copy(qfB[:, 0:TG], pk(0, 0, 64))
                            return e.copy(qfB[:, TG:2 * TG], pk(1, 0, 64))
                        E_.op("act", f, outs=[QF], ins=[PB[0], PB[1]])

                        def f2(e):
                            e.copy(kvfB[:, 0:TG], pk(2, 0, 64))
                            return e.copy(kvfB[:, TG:2 * TG],
                                          pk(3, 0, 64))
                        E_.op("act", f2, outs=[KVF], ins=[PB[2], PB[3]])
                    sec("scalar", _a5)

                    if DEBUG_DUMP and g == 0 and h == 0:
                        def _dbg3(E_):
                            E_.op("dbgs", lambda e: e.dma_start(
                                out=dbg[0:64, 2048:3072], in_=kvfB[:]),
                                ins=[KVF])
                        sec("gpsimd", _dbg3)

                    def _d3(E_, h=h):
                        E_.op("dve", lambda e: e.tensor_tensor_scan(
                            kvcB[:, 0:TG], kvfB[:, 0:TG], kvfB[:, 0:TG],
                            scanc[:, h:h + 1], ALU.add, ALU.bypass),
                            outs=[KVC], ins=[KVF, SCANC])
                        E_.op("dve", lambda e: e.tensor_tensor_scan(
                            kvcB[:, TG:2 * TG], kvfB[:, TG:2 * TG],
                            kvfB[:, TG:2 * TG], scanc[:, 16 + h:17 + h],
                            ALU.add, ALU.bypass),
                            outs=[KVC], ins=[KVF, SCANC])

                        def fcar(e):
                            e.tensor_copy(scanc[:, h:h + 1],
                                          kvcB[:, TG - 1:TG])
                            return e.tensor_copy(scanc[:, 16 + h:17 + h],
                                                 kvcB[:, 2 * TG - 1:2 * TG])
                        E_.op("dve", fcar, outs=[SCANC], ins=[KVC])
                        pc_re = pcF[0:64, h:h + 1]
                        pc_im = pcF[0:64, 16 + h:17 + h]

                        def fcm(e):
                            e.scalar_tensor_tensor(
                                qvB[:, 0:TG], kvcB[:, 0:TG], pc_re,
                                qfB[:, 0:TG], ALU.add, ALU.mult)
                            e.scalar_tensor_tensor(
                                ttB[:, :], kvcB[:, TG:2 * TG], pc_im,
                                qfB[:, TG:2 * TG], ALU.add, ALU.mult)
                            e.tensor_tensor(qvB[:, 0:TG], qvB[:, 0:TG],
                                            ttB[:, :], ALU.subtract)
                            e.scalar_tensor_tensor(
                                qvB[:, TG:2 * TG], kvcB[:, TG:2 * TG], pc_im,
                                qfB[:, 0:TG], ALU.add, ALU.mult)
                            e.scalar_tensor_tensor(
                                ttB[:, :], kvcB[:, 0:TG], pc_re,
                                qfB[:, TG:2 * TG], ALU.add, ALU.mult)
                            e.tensor_tensor(qvB[:, TG:2 * TG],
                                            qvB[:, TG:2 * TG],
                                            ttB[:, :], ALU.add)
                            e.scalar_tensor_tensor(
                                qvB[0:1, 0:TG], kvcB[0:1, 0:TG],
                                pc_re[0:1, :], qfB[0:1, 0:TG],
                                ALU.add, ALU.mult)
                            return e.scalar_tensor_tensor(
                                qvB[0:1, TG:2 * TG], kvcB[0:1, TG:2 * TG],
                                pc_im[0:1, :], qfB[0:1, TG:2 * TG],
                                ALU.add, ALU.mult)
                        E_.op("dve", fcm, outs=[QV, SQB],
                              ins=[KVC, QF, PCF])
                    sec("vector", _d3)

                    if DEBUG_DUMP and g == 0 and h == 0:
                        def _dbg4(E_):
                            E_.op("dbgs", lambda e: e.dma_start(
                                out=dbg[0:64, 3072:4096], in_=kvcB[:]),
                                ins=[KVC])
                            E_.op("dbgs", lambda e: e.dma_start(
                                out=dbg[0:64, 4096:5120], in_=qvB[:]),
                                ins=[QV])
                        sec("gpsimd", _dbg4)

                    def _p7(E_):
                        def f(e):
                            e.matmul(pk(7), c_ire_b[:, :], qvB[:, 0:TG],
                                     start=True, stop=False)
                            return e.matmul(pk(7), c_iim_b[:, :],
                                            qvB[:, TG:2 * TG],
                                            start=False, stop=True)
                        E_.op("pe", f, outs=[PB[7]], ins=[QV, CONSTS])
                    sec("tensor", _p7)

                    def _d4(E_, h=h):
                        E_.op("dve", lambda e: e.tensor_tensor(
                            vals[:, h * TG:(h + 1) * TG], pk(7), rqbB[:],
                            ALU.mult), outs=[VALS], ins=[PB[7], RQB])
                    sec("vector", _d4)

                if DEBUG_DUMP and g == 0:
                    def _dbg5(E_):
                        E_.op("dbgs", lambda e: e.dma_start(
                            out=dbg[:, 5120:5632], in_=vals[:, 0:TG]),
                            ins=[VALS])
                    sec("gpsimd", _dbg5)

                for o in range(NCH):
                    par = o % 2
                    bw = 4 + par

                    def _ld(E_, o=o, par=par):
                        E_.op(f"dqk{par}", lambda e: e.dma_start(
                            out=qksl[par][:, 0:D], in_=aWo_b[o, :, :]),
                            outs=[QKSL[par]], also=[cast_ev("aWo")])
                    sec("sync", _ld)

                    def _pe(E_, o=o, par=par, bw=bw):
                        def f(e):
                            for h2 in range(NH):
                                i = e.matmul(
                                    pk(bw),
                                    qksl[par][:, h2 * 128:(h2 + 1) * 128],
                                    vals[:, h2 * TG:(h2 + 1) * TG],
                                    start=(h2 == 0), stop=(h2 == NH - 1))
                            return i
                        E_.op("pe", f, outs=[PB[bw]], ins=[QKSL[par], VALS])
                    sec("tensor", _pe)

                    def _dv(E_, g=g, o=o, bw=bw):
                        xs = x_res[:, o * T + g * TG:o * T + (g + 1) * TG]
                        E_.op("dve", lambda e: e.tensor_tensor(
                            xs, xs, pk(bw), ALU.add),
                            outs=[XRESG[g]], ins=[PB[bw], XRESG[g]])
                    sec("vector", _dv)
            ctx.close_scope([XLNB, VALS, QKSL[0], QKSL[1], QH, KVH,
                             SQB, RQB, KVF, KVC, RQR, RKR])
        _stB.close()
        finish_casts()

        if DEBUG_TAP == 1:
            def _tp(E_):
                for c in range(NCH):
                    E_.op("dbgs", lambda e, c=c: e.dma_start(
                        out=tap[c * 128:(c + 1) * 128, :],
                        in_=x_res[:, c * T:(c + 1) * T]), ins=XRESG)
            sec("gpsimd", _tp)

        def emit_ffn(F):
            _stF = ExitStack()
            _cF = lambda nm, sh, dt: _stF.enter_context(
                nc.sbuf_tensor(un(nm), sh, dt))
            xl2 = _cF(f"xl2_{F}", [128, NCH * 1024], BF16)
            hib = [_cF(f"hib_{F}{i}", [128, 1024], BF16) for i in range(2)]
            sil = _cF(f"sil_{F}", [128, 1024], BF16)
            lnb = _cF(f"lnbF{F}", [128, 4 * TG], BF16)
            lns = _cF(f"lnsF{F}", [128, 4 * TG], BF16)
            lnt = _cF(f"lntF{F}", [128, TG], F32)
            XL2 = Buf()
            HIB = [Buf(), Buf()]
            SIL = Buf()
            LNB, LNS, LNT = Buf(), Buf(), Buf()
            HSCR = [Buf() for _ in range(NI)]

            def ln_phase(hf):
                tok0 = hf * 1024
                for s_ in range(2):
                    mrow = rows[0:1, s_ * TG:(s_ + 1) * TG]
                    rrow = rows2[0:1, s_ * TG:(s_ + 1) * TG]
                    emit_ln(
                        tok0 + s_ * TG,
                        xl2[:].rearrange(
                            "p (c t) -> p c t", c=NCH)[
                            :, :, s_ * TG:(s_ + 1) * TG],
                        lnb, lns, lnt, LNB, LNS, LNT,
                        0, 1, 2, 3, mrow, rrow, dst_buf=XL2,
                        mr_key=("ffn", s_))

            def gu_phase(hf):
                with nc.sbuf_tensor(un(f"gusl_{F}"), [128, 2 * D],
                                    BF16) as g0, \
                     nc.sbuf_tensor(un(f"gusl_{F}"), [128, 2 * D],
                                    BF16) as g1:
                    gusl = [g0, g1]
                    GUSL = [Buf(), Buf()]
                    for i in range(NI):
                        par = i % 2
                        bg = 4 * par
                        bu = 4 * par + 2

                        def _ld(E_, i=i, par=par):
                            def f(e):
                                i1 = e.dma_start(out=gusl[par][:, 0:D],
                                                 in_=gw_b[i, :, :])
                                i2 = e.dma_start(
                                    out=gusl[par][:, D:2 * D],
                                    in_=uw_b[i, :, :])
                                return [i1, i2]
                            E_.op(f"dgu{par}", f, outs=[GUSL[par]],
                                  also=[cast_ev("gw"), cast_ev("uw")])
                        sec("sync", _ld)

                        def _pe(E_, i=i, par=par, bg=bg, bu=bu):
                            def f(e):
                                for s_ in range(2):
                                    for c in range(NCH):
                                        i2 = e.matmul(
                                            pk(bg + s_),
                                            gusl[par][:, c * 128:(c + 1) * 128],
                                            xl2[:, c * 1024 + s_ * TG:
                                                c * 1024 + (s_ + 1) * TG],
                                            start=(c == 0),
                                            stop=(c == NCH - 1))
                                return i2
                            E_.op("pe", f, outs=[PB[bg], PB[bg + 1]],
                                  ins=[GUSL[par], XL2])

                            def f2(e):
                                for s_ in range(2):
                                    for c in range(NCH):
                                        i2 = e.matmul(
                                            pk(bu + s_),
                                            gusl[par][:, D + c * 128:
                                                      D + (c + 1) * 128],
                                            xl2[:, c * 1024 + s_ * TG:
                                                c * 1024 + (s_ + 1) * TG],
                                            start=(c == 0),
                                            stop=(c == NCH - 1))
                                return i2
                            E_.op("pe", f2, outs=[PB[bu], PB[bu + 1]],
                                  ins=[GUSL[par], XL2])
                        sec("tensor", _pe)

                        def _ac(E_, i=i, bg=bg):
                            gs_col = vecs[:, GS0 + i:GS0 + i + 1]

                            def f(e):
                                e.activation(sil[:, 0:TG], pk(bg),
                                             AF.Silu, scale=gs_col)
                                return e.activation(sil[:, TG:1024],
                                                    pk(bg + 1), AF.Silu,
                                                    scale=gs_col)
                            E_.op("act", f, outs=[SIL],
                                  ins=[PB[bg], PB[bg + 1], VECS])
                        sec("scalar", _ac)

                        def _dv(E_, i=i, par=par, bu=bu):
                            us_col = vecs[:, US0 + i:US0 + i + 1]

                            def f(e):
                                e.scalar_tensor_tensor(
                                    hib[par][:, 0:TG], pk(bu), us_col,
                                    sil[:, 0:TG], ALU.mult, ALU.mult)
                                return e.scalar_tensor_tensor(
                                    hib[par][:, TG:1024], pk(bu + 1),
                                    us_col, sil[:, TG:1024],
                                    ALU.mult, ALU.mult)
                            E_.op("dve", f, outs=[HIB[par]],
                                  ins=[PB[bu], PB[bu + 1], SIL, VECS])
                        sec("vector", _dv)

                        def _sp(E_, i=i, par=par):
                            E_.op(f"dhs{par}", lambda e: e.dma_start(
                                out=h_scr[i, :, :], in_=hib[par][:, :]),
                                outs=[HSCR[i]], ins=[HIB[par]])
                        sec("scalar", _sp)
                    ctx.close_scope(GUSL)

            def down_phase(hf):
                with nc.sbuf_tensor(un(f"hl_{F}a"), [128, TG], BF16) as h0, \
                     nc.sbuf_tensor(un(f"hl_{F}b"), [128, TG], BF16) as h1, \
                     nc.sbuf_tensor(un(f"dsl_{F}a"), [128, 1024],
                                    BF16) as d0, \
                     nc.sbuf_tensor(un(f"dsl_{F}b"), [128, 1024],
                                    BF16) as d1:
                    hl = [h0, h1]
                    dsl = [d0, d1]
                    HL = [Buf(), Buf()]
                    DSL = [Buf(), Buf()]
                    for s_ in range(2):
                        for rnd in range(2):
                            for i in range(NI):
                                par = i % 2

                                def _ld(E_, i=i, par=par, s_=s_, rnd=rnd):
                                    E_.op(f"dhl{par}",
                                          lambda e: e.dma_start(
                                              out=hl[par][:, :],
                                              in_=h_scr[i, :, s_ * TG:
                                                        (s_ + 1) * TG]),
                                          outs=[HL[par]], ins=[HSCR[i]])
                                    E_.op(f"ddl{par}",
                                          lambda e: e.dma_start(
                                              out=dsl[par][:, :],
                                              in_=dw_b[i, :, rnd * 1024:
                                                       (rnd + 1) * 1024]),
                                          outs=[DSL[par]],
                                          also=[cast_ev("dw")])
                                sec("sync", _ld)

                                def _pe(E_, i=i, par=par):
                                    def f(e):
                                        for oo in range(8):
                                            i2 = e.matmul(
                                                pk(oo),
                                                dsl[par][:, oo * 128:
                                                         (oo + 1) * 128],
                                                hl[par][:, :],
                                                start=(i == 0),
                                                stop=(i == NI - 1))
                                        return i2
                                    E_.op("pe", f,
                                          outs=[PB[oo] for oo in range(8)],
                                          ins=[HL[par], DSL[par]])
                                sec("tensor", _pe)

                            def _dv(E_, s_=s_, rnd=rnd, hf=hf):
                                gg = hf * 2 + s_
                                for oo in range(8):
                                    o = rnd * 8 + oo
                                    ds_col = vecs[:, DS0 + o:DS0 + o + 1]
                                    xs = x_res[:, o * T + hf * 1024 +
                                               s_ * TG:
                                               o * T + hf * 1024 +
                                               (s_ + 1) * TG]
                                    E_.op(
                                        "dve",
                                        lambda e, oo=oo, ds_col=ds_col,
                                        xs=xs: e.scalar_tensor_tensor(
                                            xs, pk(oo), ds_col, xs,
                                            ALU.mult, ALU.add),
                                        outs=[XRESG[gg]],
                                        ins=[PB[oo], XRESG[gg], VECS])
                            sec("vector", _dv)
                    ctx.close_scope(HL + DSL)

            ln_phase(0)
            gu_phase(0)
            ln_phase(1)
            down_phase(0)
            gu_phase(1)
            down_phase(1)
            ctx.close_scope([XL2, HIB[0], HIB[1], SIL, LNB, LNS, LNT] + HSCR)
            _stF.close()

        emit_ffn(0)
        if DEBUG_TAP == 2:
            def _tp(E_):
                for c in range(NCH):
                    E_.op("dbgs", lambda e, c=c: e.dma_start(
                        out=tap[c * 128:(c + 1) * 128, :],
                        in_=x_res[:, c * T:(c + 1) * T]), ins=XRESG)
            sec("gpsimd", _tp)

        with (
            nc.sbuf_tensor("S_bf", [128, D], BF16) as S_bf,
            nc.sbuf_tensor("qe2c", [128, NCHK * CH], BF16) as qe2c,
            nc.sbuf_tensor("rzcA", [64, NCHK], F32) as rzcA,
            nc.sbuf_tensor("GhatT", [128, 2], F32) as GhatT,
        ):
            _stS = ExitStack()
            S_sb = _stS.enter_context(nc.sbuf_tensor("S_sb", [128, D], F32))
            SS, SBF, QE2C, RZC, GH = Buf(), Buf(), Buf(), Buf(), Buf()

            def _z(E_):
                E_.op("dve", lambda e: e.memset(S_sb[:], 0.0), outs=[SS])
                E_.op("dve", lambda e: e.memset(GhatT[:], 0.0), outs=[GH])
            sec("vector", _z)

            OSCR = [Buf() for _ in range(NCHK)]
            GOXS = [Buf() for _ in range(NG)]
            X3S = [Buf() for _ in range(NG)]

            with nc.sbuf_tensor("xlnC", [128, NCH * TG], BF16) as xlnC:
                XLC = Buf()
                for g in range(NG):
                    with (
                        nc.sbuf_tensor(un("lnbC"), [128, 4 * TG], BF16) as lnb,
                        nc.sbuf_tensor(un("lnsC"), [128, 4 * TG], BF16) as lns,
                        nc.sbuf_tensor(un("lntC"), [128, TG], F32) as lnt,
                    ):
                        LNB, LNS, LNT = Buf(), Buf(), Buf()
                        emit_ln(g * TG,
                                xlnC[:].rearrange("p (c t) -> p c t", c=NCH),
                                lnb, lns, lnt, LNB, LNS, LNT,
                                0, 1, 2, 3,
                                rows[0:1, 0:TG], rows[0:1, TG:2 * TG],
                                dst_buf=XLC)
                        ctx.close_scope([LNB, LNS, LNT])

                    _stW = ExitStack()
                    wigsl = [_stW.enter_context(nc.sbuf_tensor(
                        un(f"wigsl{i}"), [128, D], BF16)) for i in range(2)]
                    goxe = [_stW.enter_context(nc.sbuf_tensor(
                        un(f"goxe{i}"), [128, TG], BF16)) for i in range(2)]
                    if True:
                        WIG = [Buf(), Buf()]
                        GOXE = [Buf(), Buf()]
                        for o in range(NCH):
                            par = o % 2

                            def _ld(E_, o=o, par=par):
                                E_.op(f"dwg{par}", lambda e: e.dma_start(
                                    out=wigsl[par][:, :], in_=wig_b[o, :, :]),
                                    outs=[WIG[par]], also=[cast_ev("wig")])
                            sec("sync", _ld)

                            def _pe(E_, o=o, par=par):
                                def f(e):
                                    for c in range(NCH):
                                        i2 = e.matmul(
                                            pk(par),
                                            wigsl[par][:, c * 128:(c + 1) * 128],
                                            xlnC[:, c * TG:(c + 1) * TG],
                                            start=(c == 0),
                                            stop=(c == NCH - 1))
                                    return i2
                                E_.op("pe", f, outs=[PB[par]],
                                      ins=[WIG[par], XLC])
                            sec("tensor", _pe)

                            def _ac(E_, o=o, par=par):
                                E_.op("act", lambda e: e.copy(
                                    goxe[par][:], pk(par)),
                                    outs=[GOXE[par]], ins=[PB[par]])
                            sec("scalar", _ac)

                            def _sp2(E_, g=g, o=o, par=par):
                                E_.op(f"dgx{par}", lambda e: e.dma_start(
                                    out=gox_scr[g, :, o * TG:(o + 1) * TG],
                                    in_=goxe[par][:]),
                                    outs=[GOXS[g]], ins=[GOXE[par]])
                            sec("scalar", _sp2)
                        ctx.close_scope(WIG + GOXE)
                    _stW.close()

                    _st = ExitStack()
                    _c = lambda nm, sh, dt: _st.enter_context(
                        nc.sbuf_tensor(un(nm), sh, dt))
                    wqk = _c("wqk", [128, 2 * D], BF16)
                    ksC = _c("ksC", [128, TG], F32)
                    sf1 = _c("sf1", [128, TG], F32)
                    sf2 = _c("sf2", [128, TG], F32)
                    BhC = _c("BhC", [128, TG], F32)
                    qeC = _c("qeC", [128, TG], BF16)
                    keC = _c("keC", [128, TG], BF16)
                    kdC = _c("kdC", [128, TG], BF16)
                    eqC = qeC
                    decC = _c("decC", [128, 8], F32)
                    egC = _c("egC", [128, 1], F32)
                    _vtl0 = _c("vtl", [64, D], BF16)
                    vtl = [_vtl0, _vtl0]
                    AsbC = _c("AsbC", [64, CH], BF16)
                    kdT = _c("kdT", [64, 128], BF16)
                    och = _c("och", [64, D // 2], BF16)
                    if True:
                        WQK, KS, SF1, SF2, BH = (Buf(), Buf(), Buf(), Buf(),
                                                 Buf())
                        QE, KE, KD, DEC, EG = (Buf(), Buf(), Buf(),
                                               Buf(), Buf())
                        EQ = QE
                        _VT0 = Buf()
                        VT = [_VT0, _VT0]
                        ASB, KDT, OCH = Buf(), Buf(), Buf()
                        zrow = rows2[0:1, 0:TG]
                        ZR = Buf()

                        def _ld(E_):
                            def f(e):
                                i1 = e.dma_start(out=wqk[:, 0:D],
                                                 in_=bWq_b[:, :])
                                i2 = e.dma_start(out=wqk[:, D:2 * D],
                                                 in_=bWk_b[:, :])
                                return [i1, i2]
                            E_.op("dwq", f, outs=[WQK],
                                  also=[cast_ev("bWq"), cast_ev("bWk")])
                        sec("sync", _ld)

                        def _pe(E_):
                            def f(e):
                                for c in range(NCH):
                                    i2 = e.matmul(
                                        pk(2), wqk[:, c * 128:(c + 1) * 128],
                                        xlnC[:, c * TG:(c + 1) * TG],
                                        start=(c == 0), stop=(c == NCH - 1))
                                return i2
                            E_.op("pe", f, outs=[PB[2]], ins=[WQK, XLC])

                            def f2(e):
                                for c in range(NCH):
                                    i2 = e.matmul(
                                        pk(3),
                                        wqk[:, D + c * 128:D + (c + 1) * 128],
                                        xlnC[:, c * TG:(c + 1) * TG],
                                        start=(c == 0), stop=(c == NCH - 1))
                                return i2
                            E_.op("pe", f2, outs=[PB[3]], ins=[WQK, XLC])
                        sec("tensor", _pe)

                        def _ac(E_):
                            E_.op("act", lambda e: e.activation(
                                ksC[:], pk(3), AF.Sigmoid),
                                outs=[KS], ins=[PB[3]])
                            E_.op("act", lambda e: e.activation(
                                eqC[:], pk(2), AF.Exp),
                                outs=[EQ], ins=[PB[2]])
                        sec("scalar", _ac)

                        def _pz(E_):
                            E_.op("pe", lambda e: e.matmul(
                                pk(4, 0, 1), c_onesc[:, :], eqC[:],
                                start=True, stop=True),
                                outs=[PB[4]], ins=[EQ, CONSTS])
                        sec("tensor", _pz)

                        def _dz(E_):
                            E_.op("dve", lambda e: e.reciprocal(
                                zrow, pk(4, 0, 1)), outs=[ZR], ins=[PB[4]])
                        sec("vector", _dz)

                        def _pz2(E_):
                            def f(e):
                                for c in range(8):
                                    i2 = e.transpose(
                                        pk(5, 0, 64, c, 1),
                                        zrow[0:1, c * CH:(c + 1) * CH],
                                        c_idf[0:1, 0:1])
                                return i2
                            E_.op("pe", f, outs=[PB[5]], ins=[ZR, CONSTS])
                        sec("tensor", _pz2)

                        def _az(E_, g=g):
                            E_.op("act", lambda e: e.copy(
                                rzcA[:, g * 8:(g + 1) * 8], pk(5, 0, 64, 0, 8)),
                                outs=[RZC], ins=[PB[5]])
                        sec("scalar", _az)

                        def _a2(E_):
                            E_.op("act", lambda e: e.activation(
                                sf2[:], ksC[:], AF.Exp), outs=[SF2], ins=[KS])
                            E_.op("act", lambda e: e.activation(
                                sf2[:], sf2[:], AF.Ln, bias=c_cb[:, 1:2]),
                                outs=[SF2], ins=[SF2, CONSTS])
                        sec("scalar", _a2)

                        def _d2(E_):
                            for c in range(8):
                                E_.op("dve", lambda e, c=c: e.tensor_tensor_scan(
                                    BhC[:, c * CH:(c + 1) * CH],
                                    sf2[:, c * CH:(c + 1) * CH],
                                    sf2[:, c * CH:(c + 1) * CH],
                                    0.0, ALU.add, ALU.bypass),
                                    outs=[BH], ins=[SF2])
                            E_.op("dve", lambda e: e.tensor_tensor(
                                sf1[:], pk(2), BhC[:], ALU.subtract),
                                outs=[SF1], ins=[PB[2], BH])
                        sec("vector", _d2)

                        def _a3(E_):
                            E_.op("act", lambda e: e.activation(
                                qeC[:], sf1[:], AF.Exp), outs=[QE], ins=[SF1])
                            E_.op("act", lambda e: e.activation(
                                sf1[:], BhC[:], AF.Exp), outs=[SF1], ins=[BH])
                            E_.op("act", lambda e: e.activation(
                                decC[:, :],
                                BhC[:, :].rearrange("p (c t) -> p c t", c=8)[:, :, CH - 1],
                                AF.Exp, scale=-1.0),
                                outs=[DEC], ins=[BH])
                        sec("scalar", _a3)

                        def _d3(E_):
                            E_.op("dve", lambda e: e.tensor_tensor(
                                keC[:], ksC[:], sf1[:], ALU.mult),
                                outs=[KE], ins=[KS, SF1])
                            for c in range(8):
                                E_.op("dve", lambda e, c=c: e.tensor_scalar(
                                    kdC[:, c * CH:(c + 1) * CH],
                                    keC[:, c * CH:(c + 1) * CH],
                                    decC[:, c:c + 1], None, ALU.mult),
                                    outs=[KD], ins=[KE, DEC])
                        sec("vector", _d3)

                        for c in range(8):
                            gc = g * 8 + c

                            def _aq(E_, c=c):
                                E_.op("act", lambda e: e.activation(
                                    egC[:], GhatT[:, 0:1], AF.Exp,
                                    scale=-1.0), outs=[EG], ins=[GH])
                            sec("scalar", _aq)

                            def _dq(E_, c=c, gc=gc):
                                E_.op("dve", lambda e: e.tensor_scalar(
                                    qe2c[:, gc * CH:(gc + 1) * CH],
                                    qeC[:, c * CH:(c + 1) * CH],
                                    egC[:, :], None, ALU.mult),
                                    outs=[QE2C], ins=[QE, EG])
                                E_.op("dve", lambda e: e.tensor_tensor(
                                    GhatT[:, 0:1], GhatT[:, 0:1],
                                    BhC[:, (c + 1) * CH - 1:(c + 1) * CH],
                                    ALU.add), outs=[GH], ins=[GH, BH])
                            sec("vector", _dq)

                        for c in range(8):
                            gc = g * 8 + c
                            cs = slice(c * CH, (c + 1) * CH)
                            par2 = c % 2

                            vt_pe0 = [None]

                            def _vt_pe(E_, c=c, par2=par2):
                                vt_ac0 = ctx.cnt["act"]
                                vt_pe0[0] = ctx.cnt["pe"]
                                for fc in range(16):
                                    bsel = 5 if fc % 2 == 0 else 6
                                    if fc >= 2:
                                        E_.waits(also=[("act",
                                                        vt_ac0 + fc - 1)])
                                    E_.op("pe", lambda e, fc=fc, bsel=bsel:
                                          e.transpose(
                                              pkb(bsel, 0, 64, 0, 128),
                                              xlnC[:, fc * TG + c * CH:
                                                   fc * TG + (c + 1) * CH],
                                              c_idb[:, :]),
                                          outs=[PB[bsel]], ins=[XLC, CONSTS])
                            sec("tensor", _vt_pe)

                            def _vt_ac(E_, c=c, par2=par2):
                                for fc in range(16):
                                    bsel = 5 if fc % 2 == 0 else 6
                                    E_.waits(also=[("pe",
                                                    vt_pe0[0] + fc + 1)])
                                    ev = E_.op(
                                        "act", lambda e, fc=fc, bsel=bsel:
                                        e.copy(vtl[par2][:, fc * 128:
                                                         (fc + 1) * 128],
                                               pkb(bsel, 0, 64, 0, 128)),
                                        outs=[VT[par2]])
                                    PB[bsel].rd.append(ev)
                            sec("scalar", _vt_ac)

                            def _sbf(E_):
                                E_.op("dve", lambda e: e.tensor_copy(
                                    S_bf[:], S_sb[:]), outs=[SBF], ins=[SS])
                            sec("vector", _sbf)

                            def _at(E_, c=c, cs=cs):
                                E_.op("pe", lambda e: e.matmul(
                                    pk(7, 0, 64, 0, 64), keC[:, cs],
                                    qeC[:, cs], start=True, stop=True),
                                    outs=[PB[7]], ins=[KE, QE])
                            sec("tensor", _at)

                            def _am(E_):
                                E_.op("dve", lambda e: e.tensor_tensor(
                                    AsbC[:, :], pk(7, 0, 64, 0, 64),
                                    c_triu[:, :], ALU.mult),
                                    outs=[ASB], ins=[PB[7], CONSTS])
                            sec("vector", _am)

                            def _kt(E_, cs=cs):
                                E_.op("pe", lambda e: e.transpose(
                                    pkb(7, 0, 64, 0, 128), kdC[:, cs],
                                    c_idb[:, :]),
                                    outs=[PB[7]], ins=[KD, CONSTS])
                            sec("tensor", _kt)

                            def _ke(E_):
                                E_.op("act", lambda e: e.copy(
                                    kdT[:, :], pkb(7, 0, 64, 0, 128)),
                                    outs=[KDT], ins=[PB[7]])
                            sec("scalar", _ke)

                            def _omm(E_, c=c, cs=cs, par2=par2):
                                def f(e):
                                    for dvs in range(4):
                                        e.matmul(
                                            pk(dvs, 0, 64), AsbC[:, :],
                                            vtl[par2][:, dvs * TG:
                                                      (dvs + 1) * TG],
                                            start=True, stop=False)
                                        i2 = e.matmul(
                                            pk(dvs, 0, 64), qeC[:, cs],
                                            S_bf[:, dvs * TG:(dvs + 1) * TG],
                                            start=False, stop=True)
                                    return i2
                                E_.op("pe", f,
                                      outs=[PB[0], PB[1], PB[2], PB[3]],
                                      ins=[ASB, VT[par2], QE, SBF])

                                def f2(e):
                                    for dvs in range(4):
                                        i2 = e.matmul(
                                            pk(4 + dvs), kdT[:, :],
                                            vtl[par2][:, dvs * TG:
                                                      (dvs + 1) * TG],
                                            start=True, stop=True)
                                    return i2
                                E_.op("pe", f2,
                                      outs=[PB[4], PB[5], PB[6], PB[7]],
                                      ins=[KDT, VT[par2]])
                            sec("tensor", _omm)

                            def _su(E_, c=c):
                                E_.op("dve", lambda e: e.scalar_tensor_tensor(
                                    S_sb[:], S_sb[:], decC[:, c:c + 1],
                                    pfull[:, 4 * 512:8 * 512], ALU.mult,
                                    ALU.add),
                                    outs=[SS],
                                    ins=[SS, DEC, PB[4], PB[5], PB[6], PB[7]])
                            sec("vector", _su)

                            for piece in range(2):
                                def _oe(E_, gc=gc, piece=piece):
                                    E_.op("act", lambda e: e.activation(
                                        och[:, :],
                                        pfull[0:64, piece * 1024:
                                              piece * 1024 + 1024],
                                        AF.Copy,
                                        scale=rzcA[0:64, gc:gc + 1]),
                                        outs=[OCH],
                                        ins=[PB[2 * piece], PB[2 * piece + 1],
                                             RZC])
                                sec("scalar", _oe)

                                def _os(E_, gc=gc, piece=piece):
                                    E_.op("dos", lambda e: e.dma_start(
                                        out=o_scr[gc, :, piece * 1024:
                                                  piece * 1024 + 1024],
                                        in_=och[:, :]),
                                        outs=[OSCR[gc]], ins=[OCH])
                                sec("scalar", _os)
                        ctx.close_scope([WQK, KS, SF1, SF2, BH, QE, KE, KD,
                                         EQ, DEC, EG, VT[0], VT[1], ASB, KDT,
                                         OCH, ZR])
                    _st.close()
                ctx.close_scope([XLC])

            def _sc(E_):
                ev1 = E_.op("dcc", lambda e: e.dma_start(
                    out=ccs_in[:, :], in_=S_sb[:]), ins=[SS])
                ev2 = E_.op("cc", lambda e: e.collective_compute(
                    "AllGather", ALU.bypass,
                    replica_groups=[[0, 1], [2, 3], [4, 5], [6, 7]],
                    ins=[ccs_in[:, :].opt()], outs=[ccs_out[:, :].opt()]),
                    cc=True, also=[ev1])
                E_.op("dcc", lambda e: e.dma_start(out=S_sb[:],
                                                     in_=ccs_out[0:128, :]),
                      outs=[SS], also=[ev2])
            sec("gpsimd", _sc)

            def _sm(E_):
                E_.op("dve", lambda e: e.tensor_scalar(
                    S_bf[:], S_sb[:], c_pmask[:, :], None, ALU.mult),
                    outs=[SBF], ins=[SS, CONSTS])
            sec("vector", _sm)
            ctx.close_scope([SS])
            _stS.close()

            _st3 = ExitStack()
            _c3 = lambda nm, sh, dt: _st3.enter_context(
                nc.sbuf_tensor(nm, sh, dt))
            och3 = _c3("och3", [64, D], BF16)
            ofm = _c3("ofm", [128, NCH * TG], BF16)
            sfin = _c3("sfin", [128, NCH * TG], BF16)
            goxl = [_c3(f"goxl{i}", [128, TG], BF16) for i in range(2)]
            wsl3 = [_c3(f"wsl3{i}", [128, D], BF16) for i in range(2)]
            sqf3 = _c3("sqf3", [128, TG], BF16)
            if True:
                OCH3, OFM, SFIN = Buf(), Buf(), Buf()
                GOXL = [Buf(), Buf()]
                WSL3 = [Buf(), Buf()]
                SQF3 = Buf()
                crow = rows2[0:1, TG:2 * TG]
                CROW = Buf()
                for g in range(NG):
                    for c in range(8):
                        gc = g * 8 + c

                        def _lo(E_, gc=gc):
                            E_.op("doc", lambda e: e.dma_start(
                                out=och3[:, :], in_=o_scr[gc, :, :]),
                                outs=[OCH3], ins=[OSCR[gc]])
                        sec("sync", _lo)

                        def _dc(E_, gc=gc):
                            def f(e):
                                for dvs in range(4):
                                    i2 = e.matmul(
                                        pk(dvs, 0, 64),
                                        qe2c[:, gc * CH:(gc + 1) * CH],
                                        S_bf[:, dvs * TG:(dvs + 1) * TG],
                                        start=True, stop=True)
                                return i2
                            E_.op("pe", f, outs=[PB[0], PB[1], PB[2], PB[3]],
                                  ins=[QE2C, SBF])
                        sec("tensor", _dc)

                        def _da(E_, gc=gc):
                            E_.op("dve", lambda e: e.scalar_tensor_tensor(
                                och3[:, :], pfull[0:64, 0:2048],
                                rzcA[0:64, gc:gc + 1], och3[:, :],
                                ALU.mult, ALU.add),
                                outs=[OCH3],
                                ins=[PB[0], PB[1], PB[2], PB[3], RZC, OCH3])
                        sec("vector", _da)

                        tr_pe0 = [None]

                        def _tr(E_, c=c):
                            tr_ac0 = ctx.cnt["act"]
                            tr_pe0[0] = ctx.cnt["pe"]
                            for fc in range(16):
                                bsel = 4 + (fc % 2)
                                if fc >= 2:
                                    E_.waits(also=[("act", tr_ac0 + fc - 1)])
                                E_.op("pe", lambda e, fc=fc, bsel=bsel:
                                      e.transpose(
                                          pkb(bsel, 0, 128, 0, 64),
                                          och3[:, fc * 128:(fc + 1) * 128],
                                          c_idb[0:64, 0:64]),
                                      outs=[PB[bsel]], ins=[OCH3, CONSTS])
                        sec("tensor", _tr)

                        def _te(E_, c=c):
                            for fc in range(16):
                                bsel = 4 + (fc % 2)
                                E_.waits(also=[("pe", tr_pe0[0] + fc + 1)])
                                ev = E_.op(
                                    "act", lambda e, fc=fc, bsel=bsel:
                                    e.copy(ofm[:, fc * TG + c * CH:
                                               fc * TG + (c + 1) * CH],
                                           pkb(bsel, 0, 128, 0, 64)),
                                    outs=[OFM])
                                PB[bsel].rd.append(ev)
                        sec("scalar", _te)

                    for fc in range(NCH):
                        def _s1(E_, fc=fc):
                            E_.op("act", lambda e: e.activation(
                                sqf3[:], ofm[:, fc * TG:(fc + 1) * TG],
                                AF.Square), outs=[SQF3], ins=[OFM])
                        sec("scalar", _s1)

                        def _s2(E_, fc=fc):
                            E_.op("pe", lambda e: e.matmul(
                                pk(6, 0, 1), c_onesc[:, :], sqf3[:],
                                start=(fc == 0), stop=(fc == NCH - 1)),
                                outs=[PB[6]], ins=[SQF3, CONSTS])
                        sec("tensor", _s2)

                    def _cr(E_):
                        E_.op("act", lambda e: e.activation(
                            crow, pk(6, 0, 1), AF.Sqrt, scale=1.0 / D,
                            bias=c_cb[0:1, 0:1]),
                            outs=[CROW], ins=[PB[6], CONSTS])
                    sec("scalar", _cr)

                    def _cr2(E_):
                        E_.op("dve", lambda e: e.reciprocal(crow, crow),
                              outs=[CROW], ins=[CROW])
                    sec("vector", _cr2)

                    def _cb(E_):
                        E_.op("pe", lambda e: e.matmul(
                            pk(7), c_onesr[:, :], crow, start=True,
                            stop=True), outs=[PB[7]], ins=[CROW, CONSTS])
                    sec("tensor", _cb)

                    for o in range(NCH):
                        par = o % 2

                        def _lw(E_, g=g, o=o, par=par):
                            E_.op(f"dw3{par}", lambda e: e.dma_start(
                                out=wsl3[par][:, :], in_=wog_b[o, :, :]),
                                outs=[WSL3[par]], also=[cast_ev("wog")])
                            E_.op(f"dg3{par}", lambda e: e.dma_start(
                                out=goxl[par][:, :],
                                in_=gox_scr[g, :, o * TG:(o + 1) * TG]),
                                outs=[GOXL[par]], ins=[GOXS[g]])
                        sec("sync", _lw)

                        def _pg(E_, o=o, par=par):
                            def f(e):
                                for fc in range(NCH):
                                    e.matmul(pk(par),
                                             wsl3[par][:, fc * 128:
                                                       (fc + 1) * 128],
                                             ofm[:, fc * TG:(fc + 1) * TG],
                                             start=(fc == 0), stop=False)
                                return e.matmul(pk(par), c_idb[:, :],
                                                goxl[par][:, :],
                                                start=False, stop=True)
                            E_.op("pe", f, outs=[PB[par]],
                                  ins=[WSL3[par], OFM, GOXL[par], CONSTS])
                        sec("tensor", _pg)

                        def _sg(E_, o=o, par=par):
                            E_.op("act", lambda e: e.activation(
                                sfin[:, o * TG:(o + 1) * TG], pk(par),
                                AF.Silu), outs=[SFIN], ins=[PB[par]])
                        sec("scalar", _sg)

                        def _fin(E_, o=o):
                            gn_col = vecs[:, GNW0 + o:GNW0 + o + 1]
                            sl = slice(o * TG, (o + 1) * TG)
                            E_.op("dve", lambda e: e.scalar_tensor_tensor(
                                sfin[:, sl], ofm[:, sl], gn_col,
                                sfin[:, sl], ALU.mult, ALU.mult),
                                outs=[SFIN], ins=[OFM, VECS, SFIN])
                            E_.op("dve", lambda e: e.tensor_tensor(
                                sfin[:, sl], sfin[:, sl], pk(7), ALU.mult),
                                outs=[SFIN], ins=[SFIN, PB[7]])
                        sec("vector", _fin)

                    for o in range(NCH):
                        par = o % 2

                        def _lw(E_, o=o, par=par):
                            E_.op(f"dw3{par}", lambda e: e.dma_start(
                                out=wsl3[par][:, :], in_=wob_b[o, :, :]),
                                outs=[WSL3[par]], also=[cast_ev("wob")])
                        sec("sync", _lw)

                        def _pw(E_, o=o, par=par):
                            def f(e):
                                for fc in range(NCH):
                                    i2 = e.matmul(
                                        pk(2 + par),
                                        wsl3[par][:, fc * 128:(fc + 1) * 128],
                                        sfin[:, fc * TG:(fc + 1) * TG],
                                        start=(fc == 0), stop=(fc == NCH - 1))
                                return i2
                            E_.op("pe", f, outs=[PB[2 + par]],
                                  ins=[WSL3[par], SFIN])
                        sec("tensor", _pw)

                        def _rw(E_, g=g, o=o, par=par):
                            xs = x_res[:, o * T + g * TG:o * T + (g + 1) * TG]
                            E_.op("dve", lambda e: e.tensor_tensor(
                                xs, xs, pk(2 + par), ALU.add),
                                outs=[XRESG[g]], ins=[PB[2 + par], XRESG[g]])
                        sec("vector", _rw)
                ctx.close_scope([OCH3, OFM, SFIN, GOXL[0], GOXL[1], WSL3[0],
                                 WSL3[1], SQF3, CROW])
            _st3.close()
            ctx.close_scope([SS, SBF, QE2C, RZC, GH])

        if DEBUG_TAP == 3:
            def _tp(E_):
                for c in range(NCH):
                    E_.op("dbgs", lambda e, c=c: e.dma_start(
                        out=tap[c * 128:(c + 1) * 128, :],
                        in_=x_res[:, c * T:(c + 1) * T]), ins=XRESG)
            sec("gpsimd", _tp)

        emit_ffn(1)

        def _out(E_):
            for c in range(NCH):
                E_.op("dout", lambda e, c=c: e.dma_start(
                    out=y[c * 128:(c + 1) * 128, :],
                    in_=x_res[:, c * T:(c + 1) * T]), ins=XRESG)
            E_.waits(also=[("dout", ctx.cnt["dout"]),
                           ("dbgs", ctx.cnt["dbgs"]) if ctx.cnt["dbgs"]
                           else None])
        sec("gpsimd", _out)

    es.close()
    return nc


def _consts():
    n = np.arange(128)
    m = np.arange(64)
    ang = 2 * np.pi * np.outer(n, m) / 128.0
    DRE = (np.cos(ang) * np.sqrt(2.0))
    DRE[:, 0] = 1.0
    DIM = (-np.sin(ang) * np.sqrt(2.0))
    DIM[:, 0] = (-1.0) ** n
    ang2 = 2 * np.pi * np.outer(m, n) / 128.0
    IRE = np.cos(ang2) / 128.0
    IRE[0, :] = 1.0 / 128.0
    IIM = -np.sin(ang2) / 128.0
    IIM[0, :] = ((-1.0) ** n) / 128.0
    ident = np.eye(128)
    triu = np.triu(np.ones((64, 64)))
    return (DRE.astype(np.float32), DIM.astype(np.float32),
            IRE.astype(np.float32), IIM.astype(np.float32),
            ident.astype(np.float32), triu.astype(np.float32))


def _slab_oc(W):
    O, K = W.shape
    no, nk = O // 128, K // 128
    A = W.reshape(no, 128, nk, 128)
    return np.ascontiguousarray(A.transpose(0, 3, 2, 1)).reshape(no, 128, K)


def kernel(L0_attn_a_x, L0_attn_a_W_q, L0_attn_a_W_kv, L0_attn_a_W_o,
           L0_attn_b_Wq, L0_attn_b_Wk, L0_attn_b_Wog, L0_attn_b_Wig,
           L0_attn_b_Wo, L0_attn_b_g_norm_w, L0_ffn_gate_w, L0_ffn_up_w,
           L0_ffn_down_w, L0_ffn_gate_s, L0_ffn_up_s, L0_ffn_down_s):
    x = np.asarray(L0_attn_a_x, dtype=np.float32)
    DRE, DIM, IRE, IIM, ident, triu = _consts()

    aWq_s = _slab_oc(np.asarray(L0_attn_a_W_q, np.float32))
    aWkv_s = _slab_oc(np.asarray(L0_attn_a_W_kv, np.float32))
    aWo_s = _slab_oc(np.asarray(L0_attn_a_W_o, np.float32))
    gw_s = _slab_oc(np.asarray(L0_ffn_gate_w, np.float32))
    uw_s = _slab_oc(np.asarray(L0_ffn_up_w, np.float32))
    dwT = np.ascontiguousarray(np.asarray(L0_ffn_down_w, np.float32).T)
    dw_s = np.ascontiguousarray(dwT.reshape(NI, 128, D))
    bWq_s = _slab_oc(np.asarray(L0_attn_b_Wq, np.float32))[0]
    bWk_s = _slab_oc(np.asarray(L0_attn_b_Wk, np.float32))[0]
    wog_s = _slab_oc(np.asarray(L0_attn_b_Wog, np.float32))
    wig_s = _slab_oc(np.asarray(L0_attn_b_Wig, np.float32))
    wob_s = _slab_oc(np.asarray(L0_attn_b_Wo, np.float32))
    gnw_s = np.ascontiguousarray(
        np.asarray(L0_attn_b_g_norm_w, np.float32).reshape(NCH, 128).T)
    gs_s = np.ascontiguousarray(
        np.asarray(L0_ffn_gate_s, np.float32).reshape(NI, 128).T)
    us_s = np.ascontiguousarray(
        np.asarray(L0_ffn_up_s, np.float32).reshape(NI, 128).T)
    ds_s = np.ascontiguousarray(
        np.asarray(L0_ffn_down_s, np.float32).reshape(NCH, 128).T)
    onesr = np.ones((1, 128), np.float32)
    onesc = np.ones((128, 1), np.float32)
    cbias = np.zeros((128, 2), np.float32)
    cbias[:, 0] = EPS
    cbias[:, 1] = 1.0

    nc = build_nc()
    in_maps = []
    for core in range(8):
        b, p = core // 2, core % 2
        xT = np.ascontiguousarray(x[b, p * T:(p + 1) * T, :].T)
        pmask = np.full((128, 1), float(p), np.float32)
        in_maps.append({
            "xT": xT, "dre": DRE, "dim": DIM, "ire": IRE, "iim": IIM,
            "ident": ident, "triu64": triu, "onesr": onesr, "onesc": onesc,
            "pmask": pmask, "cbias": cbias, "aWq": aWq_s, "aWkv": aWkv_s, "aWo": aWo_s,
            "gw": gw_s, "uw": uw_s, "dw": dw_s, "bWq": bWq_s, "bWk": bWk_s,
            "wog": wog_s, "wig": wig_s, "wob": wob_s, "gnw": gnw_s,
            "gsv": gs_s, "usv": us_s, "dsv": ds_s,
        })
    res = run_bass_kernel_spmd(nc, in_maps, core_ids=list(range(8)))
    out = np.empty((4, 4096, D), np.float32)
    for core in range(8):
        b, p = core // 2, core % 2
        out[b, p * T:(p + 1) * T, :] = res.results[core]["y"].T
    if DEBUG_TAP:
        kernel.taps = [res.results[c]["tap"].T for c in range(8)]
    if DEBUG_DUMP:
        kernel.dbg = [res.results[c]["dbg"] for c in range(8)]
    return out


# version 15
# speedup vs baseline: 1.0672x; 1.0672x over previous
import numpy as np
import concourse.bass as bass
import concourse.mybir as mybir
from concourse.bass_utils import run_bass_kernel_spmd
from contextlib import ExitStack

F32 = mybir.dt.float32
BF16 = mybir.dt.bfloat16
AF = mybir.ActivationFunctionType
ALU = mybir.AluOpType

D, T = 2048, 2048
NCH, NH, NI = 16, 16, 44
TG, NG = 512, 4
CH, NCHK = 64, 32
EPS = 1e-5
DMA_SEMS = ("dxa", "dxg", "dcl0", "dcl1", "dcs0", "dcs1", "dsp", "dhs0", "dhs1",
            "dos", "dgx0", "dgx1", "dqk0", "dqk1", "dxl", "dgu0", "dgu1",
            "dhl0", "dhl1", "ddl0", "ddl1", "dwq", "doc", "dw30", "dw31",
            "dg30", "dg31", "dwg0", "dwg1", "dcc", "dout", "dbgs")

DEBUG_TAP = 0
DEBUG_DUMP = 0


class Buf:
    __slots__ = ("wr", "rd")

    def __init__(self):
        self.wr = None
        self.rd = []


class Ctx:
    def __init__(self, nc, block, sems):
        self.nc = nc
        self.block = block
        self.sems = sems
        self.cnt = {k: 0 for k in sems}
        self.wm = {}
        self.gate = {}
        self.named = {}

    def close_scope(self, bufs):
        for b in bufs:
            for ev in ([b.wr] if b.wr else []) + b.rd:
                k, c = ev
                if self.gate.get(k, -1) < c:
                    self.gate[k] = c


class E:
    def __init__(self, ctx, e, eng):
        self.ctx = ctx
        self.e = e
        self.eng = eng

    def _wait(self, kind, c):
        key = (self.eng, kind)
        if self.ctx.wm.get(key, -1) >= c:
            return
        self.e.wait_ge(self.ctx.sems[kind], c)
        self.ctx.wm[key] = c

    def waits(self, outs=(), ins=(), also=()):
        for k, c in self.ctx.gate.items():
            self._wait(k, c)
        for b in outs:
            if b.wr:
                self._wait(*b.wr)
            for r in b.rd:
                self._wait(*r)
        for b in ins:
            if b.wr:
                self._wait(*b.wr)
        for ev in also:
            if ev:
                self._wait(*ev)

    def op(self, kind, fn, outs=(), ins=(), also=(), cc=False):
        self.waits(outs, ins, also)
        inst = fn(self.e)
        insts = inst if isinstance(inst, (list, tuple)) else [inst]
        for it in insts:
            if cc:
                self.ctx.cnt[kind] += 1
                it.then_inc(self.ctx.sems[kind])
            else:
                inc = 16 if kind.startswith("d") else 1
                self.ctx.cnt[kind] += inc
                it.then_inc(self.ctx.sems[kind], inc)
        ev = (kind, self.ctx.cnt[kind])
        for b in outs:
            b.wr = ev
            b.rd = []
        for b in ins:
            b.rd.append(ev)
        return ev


def build_nc():
    nc = bass.Bass("TRN2", target_bir_lowering=False)

    def ein(name, shape):
        return nc.dram_tensor(name, shape, F32, kind="ExternalInput").ap()

    xT = ein("xT", [D, T])
    dre_i = ein("dre", [128, 64])
    dim_i = ein("dim", [128, 64])
    ire_i = ein("ire", [64, 128])
    iim_i = ein("iim", [64, 128])
    ident_i = ein("ident", [128, 128])
    triu_i = ein("triu64", [64, 64])
    onesr_i = ein("onesr", [1, 128])
    onesc_i = ein("onesc", [128, 1])
    pmask_i = ein("pmask", [128, 1])
    cbias_i = ein("cbias", [128, 2])
    aWq = ein("aWq", [NH, 128, D])
    aWkv = ein("aWkv", [NH, 128, D])
    aWo = ein("aWo", [NCH, 128, D])
    gw = ein("gw", [NI, 128, D])
    uw = ein("uw", [NI, 128, D])
    dw = ein("dw", [NI, 128, D])
    bWq = ein("bWq", [128, D])
    bWk = ein("bWk", [128, D])
    wog = ein("wog", [NCH, 128, D])
    wig = ein("wig", [NCH, 128, D])
    wob = ein("wob", [NCH, 128, D])
    gnw = ein("gnw", [128, NCH])
    gsv = ein("gsv", [128, NI])
    usv = ein("usv", [128, NI])
    dsv = ein("dsv", [128, NCH])

    y = nc.dram_tensor("y", [D, T], F32, kind="ExternalOutput").ap()
    dbg = (nc.dram_tensor("dbg", [128, 6144], F32, kind="ExternalOutput").ap()
           if DEBUG_DUMP else None)
    dbgw = (nc.dram_tensor("dbgw", [16, 128, 2048], BF16,
                           kind="ExternalOutput").ap()
            if DEBUG_DUMP else None)
    tap = (nc.dram_tensor("tap", [D, T], F32, kind="ExternalOutput").ap()
           if DEBUG_TAP else None)

    scrd = lambda n, s: nc.dram_tensor(n, s, BF16).ap()
    aWq_b = scrd("aWq_b", [NH, 128, D])
    aWkv_b = scrd("aWkv_b", [NH, 128, D])
    aWo_b = scrd("aWo_b", [NCH, 128, D])
    gw_b = scrd("gw_b", [NI, 128, D])
    uw_b = scrd("uw_b", [NI, 128, D])
    dw_b = scrd("dw_b", [NI, 128, D])
    bWq_b = scrd("bWq_b", [128, D])
    bWk_b = scrd("bWk_b", [128, D])
    wog_b = scrd("wog_b", [NCH, 128, D])
    wig_b = scrd("wig_b", [NCH, 128, D])
    wob_b = scrd("wob_b", [NCH, 128, D])
    x1_scr = scrd("x1_scr", [NG, 128, NCH * TG])
    x3_scr = scrd("x3_scr", [NG, 128, NCH * TG])
    h_scr = scrd("h_scr", [NI, 128, 1024])
    o_scr = scrd("o_scr", [NCHK, 64, D])
    gox_scr = scrd("gox_scr", [NG, 128, NCH * TG])
    warm_scr = scrd("warm_scr", [128, 1024])
    cca_in = nc.dram_tensor("cca_in", [128, 16], F32).ap()
    cca_out = nc.dram_tensor("cca_out", [256, 16], F32).ap()
    ccs_in = nc.dram_tensor("ccs_in", [128, D], F32).ap()
    ccs_out = nc.dram_tensor("ccs_out", [256, D], F32).ap()

    es = ExitStack()
    sems = {k: es.enter_context(nc.semaphore(k)) for k in
            (("pe", "act", "dve", "gp", "cc") + DMA_SEMS)}

    BUFS = {}
    _ncnt = [0]

    def un(base):
        _ncnt[0] += 1
        return f"{base}_{_ncnt[0]}"

    def sb(name, shape, dtype, **kw):
        t = es.enter_context(nc.sbuf_tensor(name, shape, dtype, **kw))
        BUFS[name] = Buf()
        return t

    pfull = es.enter_context(nc.psum_tensor("pfull", [128, 4096], F32))
    PB = [Buf() for _ in range(8)]

    def pk(bank, p0=0, pl=128, c0=0, cl=512):
        return pfull[p0:p0 + pl, bank * 512 + c0:bank * 512 + c0 + cl]

    def pkb(bank, p0=0, pl=128, c0=0, cl=1024):
        return pfull[:, bank * 512:(bank + 1) * 512].bitcast(BF16)[
            p0:p0 + pl, c0:c0 + cl]

    x_res = sb("x_res", [128, NCH * T], F32)
    c_dre = sb("c_dre", [128, 64], F32)
    c_dim = sb("c_dim", [128, 64], F32)
    c_onescf = sb("c_onescf", [128, 1], F32)
    c_dre_b = sb("c_dre_b", [128, 64], BF16)
    c_dim_b = sb("c_dim_b", [128, 64], BF16)
    c_ire_b = sb("c_ire_b", [64, 128], BF16)
    c_iim_b = sb("c_iim_b", [64, 128], BF16)
    c_onesr_b = sb("c_onesr_b", [1, 128], BF16)
    c_idb = sb("c_idb", [128, 128], BF16)
    c_idf = sb("c_idf", [1, 1], F32)
    c_triu = sb("c_triu", [64, 64], F32)
    c_onesr = sb("c_onesr", [1, 128], F32)
    c_onesc = sb("c_onesc", [128, 1], BF16)
    c_pmask = sb("c_pmask", [128, 1], F32)
    c_cb = sb("c_cb", [128, 2], F32)
    rows = sb("rows", [1, 1024], F32)
    rows2 = sb("rows2", [1, 1024], F32)
    rowsb = sb("rowsb", [1, 2048], BF16)
    carryA = sb("carryA", [128, 16], F32)
    carryBt = sb("carryBt", [128, 16], BF16)
    scanc = sb("scanc", [64, 32], F32)
    pcF = sb("pcF", [64, 32], F32)
    vecs = sb("vecs", [128, NCH + 2 * NI + NCH], F32)
    GNW0, GS0, US0, DS0 = 0, NCH, NCH + NI, NCH + 2 * NI
    _stW = ExitStack()
    wstg = [_stW.enter_context(nc.sbuf_tensor(
        f"wstg{i}", [128, 512], BF16, side="right")) for i in range(2)]
    wstgf = [_stW.enter_context(nc.sbuf_tensor(
        f"wstgf{i}", [128, 512], F32, side="right")) for i in range(2)]

    B = lambda n: BUFS[n]
    CONSTS = Buf()
    ROWS = B("rows")
    ROWS2 = B("rows2")
    XRESG = [Buf() for _ in range(NG)]
    VECS = B("vecs")

    with nc.Block() as block:
        ctx = Ctx(nc, block, sems)

        def sec(eng, fn):
            getattr(block, eng)(lambda e: fn(E(ctx, e, eng)))

        def _i1(E_):
            for c in range(NCH):
                E_.op("dxa", lambda e, c=c: e.dma_start(
                    out=x_res[:, c * T:(c + 1) * T],
                    in_=xT[c * 128:(c + 1) * 128, :]), outs=XRESG)
            for dst, src in ((c_triu, triu_i),
                             (c_onesr, onesr_i), (c_pmask, pmask_i),
                             (c_cb, cbias_i)):
                pass
            E_.op("dxa", lambda e: e.dma_start(
                out=c_idf[:], in_=ident_i[0:1, 0:1]), outs=[CONSTS])
            for dst, src in ((c_triu, triu_i),
                             (c_onesr, onesr_i), (c_pmask, pmask_i),
                             (c_cb, cbias_i)):
                E_.op("dxa", lambda e, d=dst, s=src: e.dma_start(
                    out=d[:], in_=s[:, :]), outs=[CONSTS])
            for off, src, n in ((GNW0, gnw, NCH), (GS0, gsv, NI),
                                (US0, usv, NI), (DS0, dsv, NCH)):
                E_.op("dxa", lambda e, o=off, s=src, n=n: e.dma_start(
                    out=vecs[:, o:o + n], in_=s[:, :]), outs=[VECS])
        sec("sync", _i1)

        def _i2(E_):
            for dst, src in ((c_idb, ident_i), (c_onesc, onesc_i),
                             (c_dre_b, dre_i), (c_dim_b, dim_i),
                             (c_ire_b, ire_i), (c_iim_b, iim_i),
                             (c_onesr_b, onesr_i)):
                E_.op("dxg", lambda e, d=dst, s=src: e.dma_start(
                    out=d[:], in_=s[:, :]))
        sec("gpsimd", _i2)
        ctx.gate["dxg"] = ctx.cnt["dxg"]

        def _i2b(E_):
            for dst, src in ((c_dre, dre_i), (c_dim, dim_i),
                             (c_onescf, onesc_i)):
                E_.op("dxa", lambda e, d=dst, s=src: e.dma_start(
                    out=d[:], in_=s[:, :]), outs=[CONSTS])
        sec("sync", _i2b)

        CARA, CARB = B("carryA"), B("carryBt")
        SCANC, PCF = B("scanc"), B("pcF")

        def _i3(E_):
            E_.op("dve", lambda e: e.memset(carryA[:], 0.0), outs=[CARA])
            E_.op("dve", lambda e: e.memset(scanc[:], 0.0), outs=[SCANC])
            E_.op("dve", lambda e: e.memset(pcF[:], 0.0), outs=[PCF])
        sec("vector", _i3)

        cast_specs = [("aWkv", aWkv, aWkv_b, NH), ("aWq", aWq, aWq_b, NH),
                      ("aWo", aWo, aWo_b, NCH), ("gw", gw, gw_b, NI),
                      ("uw", uw, uw_b, NI), ("dw", dw, dw_b, NI),
                      ("wig", wig, wig_b, NCH), ("bWq", bWq, bWq_b, 0),
                      ("bWk", bWk, bWk_b, 0), ("wog", wog, wog_b, NCH),
                      ("wob", wob, wob_b, NCH)]
        jobs = [("warm", aWkv[0, :, q * 512:(q + 1) * 512],
                 warm_scr[:, 0:512], False) for q in range(4)]
        for nm, src, dst, n in cast_specs:
            if n == 0:
                for q in range(4):
                    jobs.append((nm, src[:, q * 512:(q + 1) * 512],
                                 dst[:, q * 512:(q + 1) * 512], q == 3))
            else:
                for i in range(n):
                    for q in range(4):
                        jobs.append((nm,
                                     src[i, :, q * 512:(q + 1) * 512],
                                     dst[i, :, q * 512:(q + 1) * 512],
                                     i == n - 1 and q == 3))
        act_c0 = [ctx.cnt["act"]]

        dxa_init = ctx.cnt["dxa"]
        from collections import defaultdict
        last_store = defaultdict(lambda: [None, None])
        cast_copy_ev = {}
        cast_pos = [0]

        def emit_cast(njobs):
            j0 = cast_pos[0]
            j1 = min(j0 + njobs, len(jobs))
            cast_pos[0] = j1
            if j1 <= j0:
                return

            act_base = ctx.cnt["act"]

            def _cl(E_):
                if j0 == 0:
                    E_.waits(also=[("dxa", dxa_init)])
                for j in range(j0, j1):
                    nm, s, d, last = jobs[j]
                    p = j % 2
                    if j - 2 >= j0:
                        E_.waits(also=[("act", act_base + (j - 2 - j0) + 1)])
                    elif j >= 2:
                        E_.waits(also=[cast_copy_ev[j - 2]])
                    E_.op(f"dcl{p}", lambda e, s=s: e.dma_start(
                        out=wstgf[p][:], in_=s))
            sec("sync", _cl)

            def _cs(E_):
                assert ctx.cnt["act"] == act_base
                for j in range(j0, j1):
                    nm, s, d, last = jobs[j]
                    p = j % 2
                    E_.waits(also=[(f"dcl{p}", 16 * (j // 2 + 1))])
                    if j >= 2:
                        E_.waits(also=[(f"dcs{p}", 16 * (j // 2))])
                    ev = E_.op("act", lambda e, p=p: e.copy(wstg[p][:],
                                                            wstgf[p][:]))
                    cast_copy_ev[j] = ev
                    ev2 = E_.op(f"dcs{p}", lambda e, d=d, p=p: e.dma_start(
                        out=d, in_=wstg[p][:]), also=[ev])
                    last_store[nm][p] = ev2
                    if last:
                        ctx.named["cast_" + nm] = [
                            e for e in last_store[nm] if e is not None]
            sec("scalar", _cs)

        def finish_casts():
            emit_cast(len(jobs))
            for k in ("dcs0", "dcs1"):
                ctx.gate[k] = max(ctx.gate.get(k, 0), ctx.cnt[k])
            _stW.close()

        emit_cast(68)

        def cast_ev(nm):
            return ctx.named["cast_" + nm]

        LN_MR = {}

        def emit_ln(tok0, dst_view, lnb, lns, lntmp, LNB, LNS, LNT,
                    bank_s, bank_q, bank_m, bank_r, mrow, rrow,
                    dst_buf=None, mr_key=0):
            XRES = XRESG[tok0 // TG]
            MR = LN_MR.setdefault("all", Buf())
            for half in range(4):
                c0, c1 = half * 4, half * 4 + 4

                def _dv(E_, c0=c0, c1=c1):
                    for c in range(c0, c1):
                        E_.op("dve", lambda e, c=c: e.tensor_copy(
                            lnb[:, (c % 4) * TG:(c % 4 + 1) * TG],
                            x_res[:, c * T + tok0:c * T + tok0 + TG]),
                            outs=[LNB], ins=[XRES])
                sec("vector", _dv)

                def _ac(E_, c0=c0, c1=c1):
                    for c in range(c0, c1):
                        E_.op("act", lambda e, c=c: e.activation(
                            lns[:, (c % 4) * TG:(c % 4 + 1) * TG],
                            x_res[:, c * T + tok0:c * T + tok0 + TG],
                            AF.Square), outs=[LNS], ins=[XRES])
                sec("scalar", _ac)

                def _pe(E_, half=half):
                    def f(e):
                        for c in range(4):
                            i = e.matmul(pk(bank_s, 0, 1), c_onesc[:, :],
                                         lnb[:, c * TG:(c + 1) * TG],
                                         start=(half == 0 and c == 0),
                                         stop=(half == 3 and c == 3))
                        return i
                    E_.op("pe", f, outs=[PB[bank_s]], ins=[LNB, CONSTS])

                    def f2(e):
                        for c in range(4):
                            i = e.matmul(pk(bank_q, 0, 1), c_onesc[:, :],
                                         lns[:, c * TG:(c + 1) * TG],
                                         start=(half == 0 and c == 0),
                                         stop=(half == 3 and c == 3))
                        return i
                    E_.op("pe", f2, outs=[PB[bank_q]], ins=[LNS, CONSTS])
                sec("tensor", _pe)

            def _a2(E_):
                E_.op("act", lambda e: e.activation(
                    mrow, pk(bank_s, 0, 1), AF.Copy, scale=1.0 / D),
                    outs=[MR], ins=[PB[bank_s]])
                E_.op("act", lambda e: e.activation(
                    rrow, pk(bank_s, 0, 1), AF.Square, scale=1.0 / D),
                    outs=[MR], ins=[PB[bank_s]])
            sec("scalar", _a2)

            def _d2(E_):
                E_.op("dve", lambda e: e.scalar_tensor_tensor(
                    rrow, pk(bank_q, 0, 1), 1.0 / D, rrow,
                    ALU.mult, ALU.subtract), outs=[MR], ins=[PB[bank_q], MR])
            sec("vector", _d2)

            def _a3(E_):
                E_.op("act", lambda e: e.activation(rrow, rrow, AF.Sqrt,
                                                    bias=c_cb[0:1, 0:1]),
                      outs=[MR], ins=[MR, CONSTS])
            sec("scalar", _a3)

            def _d3(E_):
                E_.op("dve", lambda e: e.reciprocal(rrow, rrow),
                      outs=[MR], ins=[MR])
            sec("vector", _d3)

            def _p2(E_):
                E_.op("pe", lambda e: e.matmul(pk(bank_m), c_onesr[:, :],
                                               mrow, start=True, stop=True),
                      outs=[PB[bank_m]], ins=[MR, CONSTS])
                E_.op("pe", lambda e: e.matmul(pk(bank_r), c_onesr[:, :],
                                               rrow, start=True, stop=True),
                      outs=[PB[bank_r]], ins=[MR, CONSTS])
            sec("tensor", _p2)

            DSTB = dst_buf if dst_buf is not None else Buf()

            def _d4(E_):
                for c in range(NCH):
                    E_.op("dve", lambda e, c=c: e.tensor_tensor(
                        lntmp[:, :],
                        x_res[:, c * T + tok0:c * T + tok0 + TG],
                        pk(bank_m), ALU.subtract),
                        outs=[LNT], ins=[XRES, PB[bank_m]])
                    E_.op("dve", lambda e, c=c: e.tensor_tensor(
                        dst_view[:, c, :], lntmp[:, :], pk(bank_r), ALU.mult),
                        outs=[DSTB], ins=[LNT, PB[bank_r]])
            sec("vector", _d4)
            return DSTB

        _stA = ExitStack()
        _cA = lambda nm, sh, dt: _stA.enter_context(nc.sbuf_tensor(nm, sh, dt))
        xlnA = _cA("xlnA", [128, NCH * TG], BF16)
        kvsl = [_cA(f"kvsl{i}", [128, D], BF16) for i in range(2)]
        kvbf = _cA("kvbf", [128, TG], BF16)
        sqh = _cA("sqh", [128, TG], BF16)
        kvn = _cA("kvn", [128, TG], BF16)
        accA = _cA("accA", [128, 1], F32)
        if True:
            XLN, KVSL0, KVSL1 = Buf(), Buf(), Buf()
            KVBF, SQH, KVN, ACC = Buf(), Buf(), Buf(), Buf()
            KVSL = [KVSL0, KVSL1]
            mrow, rrow = rows[0:1, 0:TG], rows[0:1, TG:2 * TG]
            rkrow = rows2[0:1, 0:TG]
            RK = Buf()
            for g in range(NG):
                with (
                    nc.sbuf_tensor(un("lnbA"), [128, 4 * TG], BF16) as lnb,
                    nc.sbuf_tensor(un("lnsA"), [128, 4 * TG], BF16) as lns,
                    nc.sbuf_tensor(un("lntA"), [128, TG], F32) as lnt,
                ):
                    LNB, LNS, LNT = Buf(), Buf(), Buf()
                    emit_ln(g * TG,
                            xlnA[:].rearrange("p (c t) -> p c t", c=NCH),
                            lnb, lns, lnt, LNB, LNS, LNT,
                            0, 1, 2, 3, mrow, rrow, dst_buf=XLN)
                    ctx.close_scope([LNB, LNS, LNT])

                def _sp(E_, g=g):
                    ev = E_.op("dsp", lambda e: e.dma_start(
                        out=x1_scr[g, :, :], in_=xlnA[:]), ins=[XLN])
                    ctx.named[f"x1sp{g}"] = ev
                sec("scalar", _sp)

                for h in range(NH):
                    par = h % 2
                    bk = 4 + par

                    def _ld(E_, h=h, par=par):
                        E_.op(f"dqk{par}", lambda e: e.dma_start(
                            out=kvsl[par][:, :], in_=aWkv_b[h, :, :]),
                            outs=[KVSL[par]], also=cast_ev("aWkv"))
                    sec("sync", _ld)
                    emit_cast(4)

                    def _pe(E_, h=h, par=par, bk=bk):
                        def f(e):
                            for c in range(NCH):
                                i = e.matmul(pk(bk),
                                             kvsl[par][:, c * 128:(c + 1) * 128],
                                             xlnA[:, c * TG:(c + 1) * TG],
                                             start=(c == 0),
                                             stop=(c == NCH - 1))
                            return i
                        E_.op("pe", f, outs=[PB[bk]], ins=[KVSL[par], XLN])
                    sec("tensor", _pe)

                    def _ac(E_, bk=bk):
                        E_.op("act", lambda e: e.copy(kvbf[:], pk(bk)),
                              outs=[KVBF], ins=[PB[bk]])
                        E_.op("act", lambda e: e.activation(
                            sqh[:], kvbf[:], AF.Square),
                            outs=[SQH], ins=[KVBF])
                    sec("scalar", _ac)

                    def _p2(E_):
                        E_.op("pe", lambda e: e.matmul(
                            pk(6, 0, 1), c_onesc[:, :], sqh[:],
                            start=True, stop=True),
                            outs=[PB[6]], ins=[SQH, CONSTS])
                    sec("tensor", _p2)

                    def _a2(E_):
                        E_.op("act", lambda e: e.activation(
                            rkrow, pk(6, 0, 1), AF.Sqrt, scale=128.0),
                            outs=[RK], ins=[PB[6]])
                    sec("scalar", _a2)

                    def _d1(E_):
                        E_.op("dve", lambda e: e.reciprocal(rkrow, rkrow),
                              outs=[RK], ins=[RK])
                    sec("vector", _d1)

                    def _a2b(E_):
                        E_.op("act", lambda e: e.copy(
                            rowsb[0:1, 1024:1024 + TG], rkrow),
                            outs=[RK], ins=[RK])
                    sec("scalar", _a2b)

                    def _p3(E_):
                        E_.op("pe", lambda e: e.matmul(
                            pk(7), c_onesr_b[:, :],
                            rowsb[0:1, 1024:1024 + TG],
                            start=True, stop=True),
                            outs=[PB[7]], ins=[RK, CONSTS])
                    sec("tensor", _p3)

                    def _d2(E_, g=g, h=h):
                        E_.op("dve", lambda e: e.scalar_tensor_tensor(
                            kvn[:], kvbf[:], 1.0, pk(7),
                            ALU.mult, ALU.mult, accum_out=accA[:, :]),
                            outs=[KVN, ACC], ins=[KVBF, PB[7]])
                        E_.op("dve", lambda e: e.tensor_tensor(
                            carryA[:, h:h + 1], carryA[:, h:h + 1],
                            accA[:, :], ALU.add),
                            outs=[CARA], ins=[ACC, CARA])
                    sec("vector", _d2)
            ctx.close_scope([XLN, KVSL0, KVSL1, KVBF, SQH, KVN, ACC, RK])
        _stA.close()

        def _cc(E_):
            ev1 = E_.op("dcc", lambda e: e.dma_start(
                out=cca_in[:, :], in_=carryA[:]), ins=[CARA])
            ev2 = E_.op("cc", lambda e: e.collective_compute(
                "AllGather", ALU.bypass,
                replica_groups=[[0, 1], [2, 3], [4, 5], [6, 7]],
                ins=[cca_in[:, :].opt()], outs=[cca_out[:, :].opt()]),
                cc=True, also=[ev1])
            E_.op("dcc", lambda e: e.dma_start(out=carryA[:],
                                                 in_=cca_out[0:128, :]),
                  outs=[CARA], also=[ev2])
        sec("gpsimd", _cc)

        def _ccm(E_):
            E_.op("dve", lambda e: e.tensor_scalar(
                carryA[:], carryA[:], c_pmask[:, :], None, ALU.mult),
                outs=[CARA], ins=[CARA, CONSTS])
        sec("vector", _ccm)

        def _ccp(E_):
            E_.op("pe", lambda e: e.matmul(pk(0, 0, 64, 0, 16), c_dre[:, :],
                                           carryA[:], start=True, stop=True),
                  outs=[PB[0]], ins=[CARA, CONSTS])
            E_.op("pe", lambda e: e.matmul(pk(0, 0, 64, 16, 16), c_dim[:, :],
                                           carryA[:], start=True, stop=True),
                  outs=[PB[0]], ins=[CARA, CONSTS])
        sec("tensor", _ccp)

        def _cce(E_):
            E_.op("act", lambda e: e.copy(pcF[:], pk(0, 0, 64, 0, 32)),
                  outs=[PCF], ins=[PB[0]])
        sec("scalar", _cce)

        _stB = ExitStack()
        _cB = lambda nm, sh, dt: _stB.enter_context(nc.sbuf_tensor(nm, sh, dt))
        xlnB = _cB("xlnB", [128, NCH * TG], BF16)
        vals = _cB("vals", [128, NH * TG], BF16)
        qksl = [_cB(f"qksl{i}", [128, 2 * D], BF16) for i in range(2)]
        qhB = _cB("qhB", [128, TG], BF16)
        kvhB = _cB("kvhB", [128, TG], BF16)
        sqhB = _cB("sqhB", [128, TG], BF16)
        rqbB = _cB("rqbB", [128, TG], BF16)
        qfB = _cB("qfB", [64, 2 * TG], BF16)
        kvfB = _cB("kvfB", [64, 2 * TG], BF16)
        kvcB = _cB("kvcB", [64, 2 * TG], BF16)
        ttB = sqhB[0:64, 0:TG]
        qvB = kvfB
        if True:
            XLNB, VALS = Buf(), Buf()
            QKSL = [Buf(), Buf()]
            QH, KVH, SQB, RQB = Buf(), Buf(), Buf(), Buf()
            QF, KVF, KVC = Buf(), Buf(), Buf()
            QV = KVF
            rq_row = rows2[0:1, 0:TG]
            rk_row = rows2[0:1, TG:2 * TG]
            rqb_row = rowsb[0:1, 1024:1024 + TG]
            rkb_row = rowsb[0:1, 1536:1536 + TG]
            RQR, RKR = Buf(), Buf()
            for g in range(NG):
                def _lx(E_, g=g):
                    E_.op("dxl", lambda e: e.dma_start(
                        out=xlnB[:], in_=x1_scr[g, :, :]),
                        outs=[XLNB], also=[ctx.named[f"x1sp{g}"]])
                sec("sync", _lx)

                for h in range(NH):
                    par = h % 2
                    bq = 4 + par

                    def _ld(E_, h=h, par=par):
                        def f(e):
                            i1 = e.dma_start(out=qksl[par][:, 0:D],
                                             in_=aWq_b[h, :, :])
                            i2 = e.dma_start(out=qksl[par][:, D:2 * D],
                                             in_=aWkv_b[h, :, :])
                            return [i1, i2]
                        E_.op(f"dqk{par}", f, outs=[QKSL[par]],
                              also=cast_ev("aWq"))
                    sec("sync", _ld)
                    emit_cast(12)

                    def _pe(E_, h=h, par=par, bq=bq):
                        def f(e):
                            for c in range(NCH):
                                i = e.matmul(pk(bq),
                                             qksl[par][:, c * 128:(c + 1) * 128],
                                             xlnB[:, c * TG:(c + 1) * TG],
                                             start=(c == 0),
                                             stop=(c == NCH - 1))
                            return i
                        E_.op("pe", f, outs=[PB[bq]], ins=[QKSL[par], XLNB])

                        def f2(e):
                            for c in range(NCH):
                                i = e.matmul(
                                    pk(6),
                                    qksl[par][:, D + c * 128:D + (c + 1) * 128],
                                    xlnB[:, c * TG:(c + 1) * TG],
                                    start=(c == 0), stop=(c == NCH - 1))
                            return i
                        E_.op("pe", f2, outs=[PB[6]], ins=[QKSL[par], XLNB])
                    sec("tensor", _pe)

                    def _ac(E_, bq=bq):
                        E_.op("act", lambda e: e.copy(qhB[:], pk(bq)),
                              outs=[QH], ins=[PB[bq]])
                        E_.op("act", lambda e: e.copy(kvhB[:], pk(6)),
                              outs=[KVH], ins=[PB[6]])
                        E_.op("act", lambda e: e.activation(
                            sqhB[:], qhB[:], AF.Square), outs=[SQB], ins=[QH])
                    sec("scalar", _ac)

                    if DEBUG_DUMP and g == 0 and h == 0:
                        def _dbg1(E_):
                            E_.op("dbgs", lambda e: e.dma_start(
                                out=dbg[:, 0:TG], in_=qhB[:]), ins=[QH])
                            E_.op("dbgs", lambda e: e.dma_start(
                                out=dbg[:, 512:1024], in_=kvhB[:]),
                                ins=[KVH])
                        sec("gpsimd", _dbg1)

                    def _p2(E_):
                        E_.op("pe", lambda e: e.matmul(
                            pk(7, 0, 1), c_onesc[:, :], sqhB[:],
                            start=True, stop=True),
                            outs=[PB[7]], ins=[SQB, CONSTS])
                    sec("tensor", _p2)

                    def _a2(E_):
                        E_.op("act", lambda e: e.activation(
                            rq_row, pk(7, 0, 1), AF.Sqrt, scale=128.0),
                            outs=[RQR], ins=[PB[7]])
                        E_.op("act", lambda e: e.activation(
                            sqhB[:], kvhB[:], AF.Square),
                            outs=[SQB], ins=[KVH])
                    sec("scalar", _a2)

                    def _p3(E_):
                        E_.op("pe", lambda e: e.matmul(
                            pk(7, 0, 1), c_onesc[:, :], sqhB[:],
                            start=True, stop=True),
                            outs=[PB[7]], ins=[SQB, CONSTS])
                    sec("tensor", _p3)

                    def _a3(E_):
                        E_.op("act", lambda e: e.activation(
                            rk_row, pk(7, 0, 1), AF.Sqrt, scale=128.0),
                            outs=[RKR], ins=[PB[7]])
                    sec("scalar", _a3)

                    def _d1(E_):
                        E_.op("dve", lambda e: e.reciprocal(rq_row, rq_row),
                              outs=[RQR], ins=[RQR])
                        E_.op("dve", lambda e: e.reciprocal(rk_row, rk_row),
                              outs=[RKR], ins=[RKR])
                    sec("vector", _d1)

                    def _a3b(E_):
                        E_.op("act", lambda e: e.copy(rqb_row, rq_row),
                              outs=[RQR], ins=[RQR])
                        E_.op("act", lambda e: e.copy(rkb_row, rk_row),
                              outs=[RKR], ins=[RKR])
                    sec("scalar", _a3b)

                    def _p4(E_):
                        E_.op("pe", lambda e: e.matmul(
                            pk(7), c_onesr_b[:, :], rkb_row,
                            start=True, stop=True),
                            outs=[PB[7]], ins=[RKR, CONSTS])
                    sec("tensor", _p4)

                    def _d2(E_):
                        E_.op("dve", lambda e: e.tensor_tensor(
                            sqhB[:], kvhB[:], pk(7), ALU.mult),
                            outs=[SQB], ins=[KVH, PB[7]])
                    sec("vector", _d2)

                    def _p5(E_):
                        E_.op("pe", lambda e: e.matmul(
                            pk(7), c_onesr_b[:, :], rqb_row,
                            start=True, stop=True),
                            outs=[PB[7]], ins=[RQR, CONSTS])
                    sec("tensor", _p5)

                    def _a4(E_):
                        E_.op("act", lambda e: e.copy(rqbB[:], pk(7)),
                              outs=[RQB], ins=[PB[7]])
                    sec("scalar", _a4)

                    if DEBUG_DUMP and g == 0 and h == 0:
                        def _dbg2(E_):
                            E_.op("dbgs", lambda e: e.dma_start(
                                out=dbg[:, 1024:1536],
                                in_=kvhB[:]), ins=[KVH])
                            E_.op("dbgs", lambda e: e.dma_start(
                                out=dbg[:, 1536:2048],
                                in_=qksl[0][:, D:D + 512]), ins=[QKSL[0]])
                        sec("gpsimd", _dbg2)

                    def _p6(E_):
                        def f(e):
                            e.matmul(pk(0, 0, 64), c_dre_b[:, :], qhB[:],
                                     start=True, stop=True)
                            e.matmul(pk(1, 0, 64), c_dim_b[:, :], qhB[:],
                                     start=True, stop=True)
                            e.matmul(pk(2, 0, 64), c_dre_b[:, :], sqhB[:],
                                     start=True, stop=True)
                            return e.matmul(pk(3, 0, 64), c_dim_b[:, :],
                                            sqhB[:], start=True, stop=True)
                        E_.op("pe", f, outs=[PB[0], PB[1], PB[2], PB[3]],
                              ins=[QH, SQB, CONSTS])
                    sec("tensor", _p6)

                    def _a5(E_):
                        def f(e):
                            e.copy(qfB[:, 0:TG], pk(0, 0, 64))
                            return e.copy(qfB[:, TG:2 * TG], pk(1, 0, 64))
                        E_.op("act", f, outs=[QF], ins=[PB[0], PB[1]])

                        def f2(e):
                            e.copy(kvfB[:, 0:TG], pk(2, 0, 64))
                            return e.copy(kvfB[:, TG:2 * TG],
                                          pk(3, 0, 64))
                        E_.op("act", f2, outs=[KVF], ins=[PB[2], PB[3]])
                    sec("scalar", _a5)

                    if DEBUG_DUMP and g == 0 and h == 0:
                        def _dbg3(E_):
                            E_.op("dbgs", lambda e: e.dma_start(
                                out=dbg[0:64, 2048:3072], in_=kvfB[:]),
                                ins=[KVF])
                        sec("gpsimd", _dbg3)

                    def _d3(E_, h=h):
                        E_.op("dve", lambda e: e.tensor_tensor_scan(
                            kvcB[:, 0:TG], kvfB[:, 0:TG], kvfB[:, 0:TG],
                            scanc[:, h:h + 1], ALU.add, ALU.bypass),
                            outs=[KVC], ins=[KVF, SCANC])
                        E_.op("dve", lambda e: e.tensor_tensor_scan(
                            kvcB[:, TG:2 * TG], kvfB[:, TG:2 * TG],
                            kvfB[:, TG:2 * TG], scanc[:, 16 + h:17 + h],
                            ALU.add, ALU.bypass),
                            outs=[KVC], ins=[KVF, SCANC])

                        def fcar(e):
                            e.tensor_copy(scanc[:, h:h + 1],
                                          kvcB[:, TG - 1:TG])
                            return e.tensor_copy(scanc[:, 16 + h:17 + h],
                                                 kvcB[:, 2 * TG - 1:2 * TG])
                        E_.op("dve", fcar, outs=[SCANC], ins=[KVC])
                        pc_re = pcF[0:64, h:h + 1]
                        pc_im = pcF[0:64, 16 + h:17 + h]

                        def fcm(e):
                            e.scalar_tensor_tensor(
                                qvB[:, 0:TG], kvcB[:, 0:TG], pc_re,
                                qfB[:, 0:TG], ALU.add, ALU.mult)
                            e.scalar_tensor_tensor(
                                ttB[:, :], kvcB[:, TG:2 * TG], pc_im,
                                qfB[:, TG:2 * TG], ALU.add, ALU.mult)
                            e.tensor_tensor(qvB[:, 0:TG], qvB[:, 0:TG],
                                            ttB[:, :], ALU.subtract)
                            e.scalar_tensor_tensor(
                                qvB[:, TG:2 * TG], kvcB[:, TG:2 * TG], pc_im,
                                qfB[:, 0:TG], ALU.add, ALU.mult)
                            e.scalar_tensor_tensor(
                                ttB[:, :], kvcB[:, 0:TG], pc_re,
                                qfB[:, TG:2 * TG], ALU.add, ALU.mult)
                            e.tensor_tensor(qvB[:, TG:2 * TG],
                                            qvB[:, TG:2 * TG],
                                            ttB[:, :], ALU.add)
                            e.scalar_tensor_tensor(
                                qvB[0:1, 0:TG], kvcB[0:1, 0:TG],
                                pc_re[0:1, :], qfB[0:1, 0:TG],
                                ALU.add, ALU.mult)
                            return e.scalar_tensor_tensor(
                                qvB[0:1, TG:2 * TG], kvcB[0:1, TG:2 * TG],
                                pc_im[0:1, :], qfB[0:1, TG:2 * TG],
                                ALU.add, ALU.mult)
                        E_.op("dve", fcm, outs=[QV, SQB],
                              ins=[KVC, QF, PCF])
                    sec("vector", _d3)

                    if DEBUG_DUMP and g == 0 and h == 0:
                        def _dbg4(E_):
                            E_.op("dbgs", lambda e: e.dma_start(
                                out=dbg[0:64, 3072:4096], in_=kvcB[:]),
                                ins=[KVC])
                            E_.op("dbgs", lambda e: e.dma_start(
                                out=dbg[0:64, 4096:5120], in_=qvB[:]),
                                ins=[QV])
                        sec("gpsimd", _dbg4)

                    def _p7(E_):
                        def f(e):
                            e.matmul(pk(7), c_ire_b[:, :], qvB[:, 0:TG],
                                     start=True, stop=False)
                            return e.matmul(pk(7), c_iim_b[:, :],
                                            qvB[:, TG:2 * TG],
                                            start=False, stop=True)
                        E_.op("pe", f, outs=[PB[7]], ins=[QV, CONSTS])
                    sec("tensor", _p7)

                    def _d4(E_, h=h):
                        E_.op("dve", lambda e: e.tensor_tensor(
                            vals[:, h * TG:(h + 1) * TG], pk(7), rqbB[:],
                            ALU.mult), outs=[VALS], ins=[PB[7], RQB])
                    sec("vector", _d4)

                if DEBUG_DUMP and g == 0:
                    def _dbg5(E_):
                        E_.op("dbgs", lambda e: e.dma_start(
                            out=dbg[:, 5120:5632], in_=vals[:, 0:TG]),
                            ins=[VALS])
                    sec("gpsimd", _dbg5)

                for o in range(NCH):
                    par = o % 2
                    bw = 4 + par

                    def _ld(E_, o=o, par=par):
                        E_.op(f"dqk{par}", lambda e: e.dma_start(
                            out=qksl[par][:, 0:D], in_=aWo_b[o, :, :]),
                            outs=[QKSL[par]], also=cast_ev("aWo"))
                    sec("sync", _ld)

                    def _pe(E_, o=o, par=par, bw=bw):
                        def f(e):
                            for h2 in range(NH):
                                i = e.matmul(
                                    pk(bw),
                                    qksl[par][:, h2 * 128:(h2 + 1) * 128],
                                    vals[:, h2 * TG:(h2 + 1) * TG],
                                    start=(h2 == 0), stop=(h2 == NH - 1))
                            return i
                        E_.op("pe", f, outs=[PB[bw]], ins=[QKSL[par], VALS])
                    sec("tensor", _pe)

                    def _dv(E_, g=g, o=o, bw=bw):
                        xs = x_res[:, o * T + g * TG:o * T + (g + 1) * TG]
                        E_.op("dve", lambda e: e.tensor_tensor(
                            xs, xs, pk(bw), ALU.add),
                            outs=[XRESG[g]], ins=[PB[bw], XRESG[g]])
                    sec("vector", _dv)
            ctx.close_scope([XLNB, VALS, QKSL[0], QKSL[1], QH, KVH,
                             SQB, RQB, KVF, KVC, RQR, RKR])
        _stB.close()
        finish_casts()

        if DEBUG_TAP == 1:
            def _tp(E_):
                for c in range(NCH):
                    E_.op("dbgs", lambda e, c=c: e.dma_start(
                        out=tap[c * 128:(c + 1) * 128, :],
                        in_=x_res[:, c * T:(c + 1) * T]), ins=XRESG)
            sec("gpsimd", _tp)

        def emit_ffn(F):
            _stF = ExitStack()
            _cF = lambda nm, sh, dt: _stF.enter_context(
                nc.sbuf_tensor(un(nm), sh, dt))
            xl2 = _cF(f"xl2_{F}", [128, NCH * 1024], BF16)
            hib = [_cF(f"hib_{F}{i}", [128, 1024], BF16) for i in range(2)]
            sil = _cF(f"sil_{F}", [128, 1024], BF16)
            lnb = _cF(f"lnbF{F}", [128, 4 * TG], BF16)
            lns = _cF(f"lnsF{F}", [128, 4 * TG], BF16)
            lnt = _cF(f"lntF{F}", [128, TG], F32)
            XL2 = Buf()
            HIB = [Buf(), Buf()]
            SIL = Buf()
            LNB, LNS, LNT = Buf(), Buf(), Buf()
            HSCR = [Buf() for _ in range(NI)]

            def ln_phase(hf):
                tok0 = hf * 1024
                for s_ in range(2):
                    mrow = rows[0:1, s_ * TG:(s_ + 1) * TG]
                    rrow = rows2[0:1, s_ * TG:(s_ + 1) * TG]
                    emit_ln(
                        tok0 + s_ * TG,
                        xl2[:].rearrange(
                            "p (c t) -> p c t", c=NCH)[
                            :, :, s_ * TG:(s_ + 1) * TG],
                        lnb, lns, lnt, LNB, LNS, LNT,
                        0, 1, 2, 3, mrow, rrow, dst_buf=XL2,
                        mr_key=("ffn", s_))

            def gu_phase(hf):
                with nc.sbuf_tensor(un(f"gusl_{F}"), [128, 2 * D],
                                    BF16) as g0, \
                     nc.sbuf_tensor(un(f"gusl_{F}"), [128, 2 * D],
                                    BF16) as g1:
                    gusl = [g0, g1]
                    GUSL = [Buf(), Buf()]
                    for i in range(NI):
                        par = i % 2
                        bg = 4 * par
                        bu = 4 * par + 2

                        def _ld(E_, i=i, par=par):
                            def f(e):
                                i1 = e.dma_start(out=gusl[par][:, 0:D],
                                                 in_=gw_b[i, :, :])
                                i2 = e.dma_start(
                                    out=gusl[par][:, D:2 * D],
                                    in_=uw_b[i, :, :])
                                return [i1, i2]
                            E_.op(f"dgu{par}", f, outs=[GUSL[par]],
                                  also=cast_ev("gw") + cast_ev("uw"))
                        sec("sync", _ld)

                        def _pe(E_, i=i, par=par, bg=bg, bu=bu):
                            def f(e):
                                for s_ in range(2):
                                    for c in range(NCH):
                                        i2 = e.matmul(
                                            pk(bg + s_),
                                            gusl[par][:, c * 128:(c + 1) * 128],
                                            xl2[:, c * 1024 + s_ * TG:
                                                c * 1024 + (s_ + 1) * TG],
                                            start=(c == 0),
                                            stop=(c == NCH - 1))
                                return i2
                            E_.op("pe", f, outs=[PB[bg], PB[bg + 1]],
                                  ins=[GUSL[par], XL2])

                            def f2(e):
                                for s_ in range(2):
                                    for c in range(NCH):
                                        i2 = e.matmul(
                                            pk(bu + s_),
                                            gusl[par][:, D + c * 128:
                                                      D + (c + 1) * 128],
                                            xl2[:, c * 1024 + s_ * TG:
                                                c * 1024 + (s_ + 1) * TG],
                                            start=(c == 0),
                                            stop=(c == NCH - 1))
                                return i2
                            E_.op("pe", f2, outs=[PB[bu], PB[bu + 1]],
                                  ins=[GUSL[par], XL2])
                        sec("tensor", _pe)

                        def _ac(E_, i=i, bg=bg):
                            gs_col = vecs[:, GS0 + i:GS0 + i + 1]

                            def f(e):
                                e.activation(sil[:, 0:TG], pk(bg),
                                             AF.Silu, scale=gs_col)
                                return e.activation(sil[:, TG:1024],
                                                    pk(bg + 1), AF.Silu,
                                                    scale=gs_col)
                            E_.op("act", f, outs=[SIL],
                                  ins=[PB[bg], PB[bg + 1], VECS])
                        sec("scalar", _ac)

                        def _dv(E_, i=i, par=par, bu=bu):
                            us_col = vecs[:, US0 + i:US0 + i + 1]

                            def f(e):
                                e.scalar_tensor_tensor(
                                    hib[par][:, 0:TG], pk(bu), us_col,
                                    sil[:, 0:TG], ALU.mult, ALU.mult)
                                return e.scalar_tensor_tensor(
                                    hib[par][:, TG:1024], pk(bu + 1),
                                    us_col, sil[:, TG:1024],
                                    ALU.mult, ALU.mult)
                            E_.op("dve", f, outs=[HIB[par]],
                                  ins=[PB[bu], PB[bu + 1], SIL, VECS])
                        sec("vector", _dv)

                        def _sp(E_, i=i, par=par):
                            E_.op(f"dhs{par}", lambda e: e.dma_start(
                                out=h_scr[i, :, :], in_=hib[par][:, :]),
                                outs=[HSCR[i]], ins=[HIB[par]])
                        sec("scalar", _sp)
                    ctx.close_scope(GUSL)

            def down_phase(hf):
                with nc.sbuf_tensor(un(f"hl_{F}a"), [128, TG], BF16) as h0, \
                     nc.sbuf_tensor(un(f"hl_{F}b"), [128, TG], BF16) as h1, \
                     nc.sbuf_tensor(un(f"dsl_{F}a"), [128, 1024],
                                    BF16) as d0, \
                     nc.sbuf_tensor(un(f"dsl_{F}b"), [128, 1024],
                                    BF16) as d1:
                    hl = [h0, h1]
                    dsl = [d0, d1]
                    HL = [Buf(), Buf()]
                    DSL = [Buf(), Buf()]
                    for s_ in range(2):
                        for rnd in range(2):
                            for i in range(NI):
                                par = i % 2

                                def _ld(E_, i=i, par=par, s_=s_, rnd=rnd):
                                    E_.op(f"dhl{par}",
                                          lambda e: e.dma_start(
                                              out=hl[par][:, :],
                                              in_=h_scr[i, :, s_ * TG:
                                                        (s_ + 1) * TG]),
                                          outs=[HL[par]], ins=[HSCR[i]])
                                    E_.op(f"ddl{par}",
                                          lambda e: e.dma_start(
                                              out=dsl[par][:, :],
                                              in_=dw_b[i, :, rnd * 1024:
                                                       (rnd + 1) * 1024]),
                                          outs=[DSL[par]],
                                          also=cast_ev("dw"))
                                sec("sync", _ld)

                                def _pe(E_, i=i, par=par):
                                    def f(e):
                                        for oo in range(8):
                                            i2 = e.matmul(
                                                pk(oo),
                                                dsl[par][:, oo * 128:
                                                         (oo + 1) * 128],
                                                hl[par][:, :],
                                                start=(i == 0),
                                                stop=(i == NI - 1))
                                        return i2
                                    E_.op("pe", f,
                                          outs=[PB[oo] for oo in range(8)],
                                          ins=[HL[par], DSL[par]])
                                sec("tensor", _pe)

                            def _dv(E_, s_=s_, rnd=rnd, hf=hf):
                                gg = hf * 2 + s_
                                for oo in range(8):
                                    o = rnd * 8 + oo
                                    ds_col = vecs[:, DS0 + o:DS0 + o + 1]
                                    xs = x_res[:, o * T + hf * 1024 +
                                               s_ * TG:
                                               o * T + hf * 1024 +
                                               (s_ + 1) * TG]
                                    E_.op(
                                        "dve",
                                        lambda e, oo=oo, ds_col=ds_col,
                                        xs=xs: e.scalar_tensor_tensor(
                                            xs, pk(oo), ds_col, xs,
                                            ALU.mult, ALU.add),
                                        outs=[XRESG[gg]],
                                        ins=[PB[oo], XRESG[gg], VECS])
                            sec("vector", _dv)
                    ctx.close_scope(HL + DSL)

            ln_phase(0)
            gu_phase(0)
            ln_phase(1)
            down_phase(0)
            gu_phase(1)
            down_phase(1)
            ctx.close_scope([XL2, HIB[0], HIB[1], SIL, LNB, LNS, LNT] + HSCR)
            _stF.close()

        emit_ffn(0)
        if DEBUG_TAP == 2:
            def _tp(E_):
                for c in range(NCH):
                    E_.op("dbgs", lambda e, c=c: e.dma_start(
                        out=tap[c * 128:(c + 1) * 128, :],
                        in_=x_res[:, c * T:(c + 1) * T]), ins=XRESG)
            sec("gpsimd", _tp)

        with (
            nc.sbuf_tensor("S_bf", [128, D], BF16) as S_bf,
            nc.sbuf_tensor("qe2c", [128, NCHK * CH], BF16) as qe2c,
            nc.sbuf_tensor("rzcA", [64, NCHK], F32) as rzcA,
            nc.sbuf_tensor("GhatT", [128, 2], F32) as GhatT,
        ):
            _stS = ExitStack()
            S_sb = _stS.enter_context(nc.sbuf_tensor("S_sb", [128, D], F32))
            SS, SBF, QE2C, RZC, GH = Buf(), Buf(), Buf(), Buf(), Buf()

            def _z(E_):
                E_.op("dve", lambda e: e.memset(S_sb[:], 0.0), outs=[SS])
                E_.op("dve", lambda e: e.memset(GhatT[:], 0.0), outs=[GH])
            sec("vector", _z)

            OSCR = [Buf() for _ in range(NCHK)]
            GOXS = [Buf() for _ in range(NG)]
            X3S = [Buf() for _ in range(NG)]

            with nc.sbuf_tensor("xlnC", [128, NCH * TG], BF16) as xlnC:
                XLC = Buf()
                for g in range(NG):
                    with (
                        nc.sbuf_tensor(un("lnbC"), [128, 4 * TG], BF16) as lnb,
                        nc.sbuf_tensor(un("lnsC"), [128, 4 * TG], BF16) as lns,
                        nc.sbuf_tensor(un("lntC"), [128, TG], F32) as lnt,
                    ):
                        LNB, LNS, LNT = Buf(), Buf(), Buf()
                        emit_ln(g * TG,
                                xlnC[:].rearrange("p (c t) -> p c t", c=NCH),
                                lnb, lns, lnt, LNB, LNS, LNT,
                                0, 1, 2, 3,
                                rows[0:1, 0:TG], rows[0:1, TG:2 * TG],
                                dst_buf=XLC)
                        ctx.close_scope([LNB, LNS, LNT])

                    _stW = ExitStack()
                    wigsl = [_stW.enter_context(nc.sbuf_tensor(
                        un(f"wigsl{i}"), [128, D], BF16)) for i in range(2)]
                    goxe = [_stW.enter_context(nc.sbuf_tensor(
                        un(f"goxe{i}"), [128, TG], BF16)) for i in range(2)]
                    if True:
                        WIG = [Buf(), Buf()]
                        GOXE = [Buf(), Buf()]
                        for o in range(NCH):
                            par = o % 2

                            def _ld(E_, o=o, par=par):
                                E_.op(f"dwg{par}", lambda e: e.dma_start(
                                    out=wigsl[par][:, :], in_=wig_b[o, :, :]),
                                    outs=[WIG[par]], also=cast_ev("wig"))
                            sec("sync", _ld)

                            def _pe(E_, o=o, par=par):
                                def f(e):
                                    for c in range(NCH):
                                        i2 = e.matmul(
                                            pk(par),
                                            wigsl[par][:, c * 128:(c + 1) * 128],
                                            xlnC[:, c * TG:(c + 1) * TG],
                                            start=(c == 0),
                                            stop=(c == NCH - 1))
                                    return i2
                                E_.op("pe", f, outs=[PB[par]],
                                      ins=[WIG[par], XLC])
                            sec("tensor", _pe)

                            def _ac(E_, o=o, par=par):
                                E_.op("act", lambda e: e.copy(
                                    goxe[par][:], pk(par)),
                                    outs=[GOXE[par]], ins=[PB[par]])
                            sec("scalar", _ac)

                            def _sp2(E_, g=g, o=o, par=par):
                                E_.op(f"dgx{par}", lambda e: e.dma_start(
                                    out=gox_scr[g, :, o * TG:(o + 1) * TG],
                                    in_=goxe[par][:]),
                                    outs=[GOXS[g]], ins=[GOXE[par]])
                            sec("scalar", _sp2)
                        ctx.close_scope(WIG + GOXE)
                    _stW.close()

                    _st = ExitStack()
                    _c = lambda nm, sh, dt: _st.enter_context(
                        nc.sbuf_tensor(un(nm), sh, dt))
                    wqk = _c("wqk", [128, 2 * D], BF16)
                    ksC = _c("ksC", [128, TG], F32)
                    sf1 = _c("sf1", [128, TG], F32)
                    sf2 = _c("sf2", [128, TG], F32)
                    BhC = _c("BhC", [128, TG], F32)
                    qeC = _c("qeC", [128, TG], BF16)
                    keC = _c("keC", [128, TG], BF16)
                    kdC = _c("kdC", [128, TG], BF16)
                    eqC = qeC
                    decC = _c("decC", [128, 8], F32)
                    egC = _c("egC", [128, 1], F32)
                    _vtl0 = _c("vtl", [64, D], BF16)
                    vtl = [_vtl0, _vtl0]
                    AsbC = _c("AsbC", [64, CH], BF16)
                    kdT = _c("kdT", [64, 128], BF16)
                    och = _c("och", [64, D // 2], BF16)
                    if True:
                        WQK, KS, SF1, SF2, BH = (Buf(), Buf(), Buf(), Buf(),
                                                 Buf())
                        QE, KE, KD, DEC, EG = (Buf(), Buf(), Buf(),
                                               Buf(), Buf())
                        EQ = QE
                        _VT0 = Buf()
                        VT = [_VT0, _VT0]
                        ASB, KDT, OCH = Buf(), Buf(), Buf()
                        zrow = rows2[0:1, 0:TG]
                        ZR = Buf()

                        def _ld(E_):
                            def f(e):
                                i1 = e.dma_start(out=wqk[:, 0:D],
                                                 in_=bWq_b[:, :])
                                i2 = e.dma_start(out=wqk[:, D:2 * D],
                                                 in_=bWk_b[:, :])
                                return [i1, i2]
                            E_.op("dwq", f, outs=[WQK],
                                  also=cast_ev("bWq") + cast_ev("bWk"))
                        sec("sync", _ld)

                        def _pe(E_):
                            def f(e):
                                for c in range(NCH):
                                    i2 = e.matmul(
                                        pk(2), wqk[:, c * 128:(c + 1) * 128],
                                        xlnC[:, c * TG:(c + 1) * TG],
                                        start=(c == 0), stop=(c == NCH - 1))
                                return i2
                            E_.op("pe", f, outs=[PB[2]], ins=[WQK, XLC])

                            def f2(e):
                                for c in range(NCH):
                                    i2 = e.matmul(
                                        pk(3),
                                        wqk[:, D + c * 128:D + (c + 1) * 128],
                                        xlnC[:, c * TG:(c + 1) * TG],
                                        start=(c == 0), stop=(c == NCH - 1))
                                return i2
                            E_.op("pe", f2, outs=[PB[3]], ins=[WQK, XLC])
                        sec("tensor", _pe)

                        def _ac(E_):
                            E_.op("act", lambda e: e.activation(
                                ksC[:], pk(3), AF.Sigmoid),
                                outs=[KS], ins=[PB[3]])
                            E_.op("act", lambda e: e.activation(
                                eqC[:], pk(2), AF.Exp),
                                outs=[EQ], ins=[PB[2]])
                        sec("scalar", _ac)

                        def _pz(E_):
                            E_.op("pe", lambda e: e.matmul(
                                pk(4, 0, 1), c_onesc[:, :], eqC[:],
                                start=True, stop=True),
                                outs=[PB[4]], ins=[EQ, CONSTS])
                        sec("tensor", _pz)

                        def _dz(E_):
                            E_.op("dve", lambda e: e.reciprocal(
                                zrow, pk(4, 0, 1)), outs=[ZR], ins=[PB[4]])
                        sec("vector", _dz)

                        def _pz2(E_):
                            def f(e):
                                for c in range(8):
                                    i2 = e.transpose(
                                        pk(5, 0, 64, c, 1),
                                        zrow[0:1, c * CH:(c + 1) * CH],
                                        c_idf[0:1, 0:1])
                                return i2
                            E_.op("pe", f, outs=[PB[5]], ins=[ZR, CONSTS])
                        sec("tensor", _pz2)

                        def _az(E_, g=g):
                            E_.op("act", lambda e: e.copy(
                                rzcA[:, g * 8:(g + 1) * 8], pk(5, 0, 64, 0, 8)),
                                outs=[RZC], ins=[PB[5]])
                        sec("scalar", _az)

                        def _a2(E_):
                            E_.op("act", lambda e: e.activation(
                                sf2[:], ksC[:], AF.Exp), outs=[SF2], ins=[KS])
                            E_.op("act", lambda e: e.activation(
                                sf2[:], sf2[:], AF.Ln, bias=c_cb[:, 1:2]),
                                outs=[SF2], ins=[SF2, CONSTS])
                        sec("scalar", _a2)

                        def _d2(E_):
                            for c in range(8):
                                E_.op("dve", lambda e, c=c: e.tensor_tensor_scan(
                                    BhC[:, c * CH:(c + 1) * CH],
                                    sf2[:, c * CH:(c + 1) * CH],
                                    sf2[:, c * CH:(c + 1) * CH],
                                    0.0, ALU.add, ALU.bypass),
                                    outs=[BH], ins=[SF2])
                            E_.op("dve", lambda e: e.tensor_tensor(
                                sf1[:], pk(2), BhC[:], ALU.subtract),
                                outs=[SF1], ins=[PB[2], BH])
                        sec("vector", _d2)

                        def _a3(E_):
                            E_.op("act", lambda e: e.activation(
                                qeC[:], sf1[:], AF.Exp), outs=[QE], ins=[SF1])
                            E_.op("act", lambda e: e.activation(
                                sf1[:], BhC[:], AF.Exp), outs=[SF1], ins=[BH])
                            E_.op("act", lambda e: e.activation(
                                decC[:, :],
                                BhC[:, :].rearrange("p (c t) -> p c t", c=8)[:, :, CH - 1],
                                AF.Exp, scale=-1.0),
                                outs=[DEC], ins=[BH])
                        sec("scalar", _a3)

                        def _d3(E_):
                            E_.op("dve", lambda e: e.tensor_tensor(
                                keC[:], ksC[:], sf1[:], ALU.mult),
                                outs=[KE], ins=[KS, SF1])
                            for c in range(8):
                                E_.op("dve", lambda e, c=c: e.tensor_scalar(
                                    kdC[:, c * CH:(c + 1) * CH],
                                    keC[:, c * CH:(c + 1) * CH],
                                    decC[:, c:c + 1], None, ALU.mult),
                                    outs=[KD], ins=[KE, DEC])
                        sec("vector", _d3)

                        for c in range(8):
                            gc = g * 8 + c

                            def _aq(E_, c=c):
                                E_.op("act", lambda e: e.activation(
                                    egC[:], GhatT[:, 0:1], AF.Exp,
                                    scale=-1.0), outs=[EG], ins=[GH])
                            sec("scalar", _aq)

                            def _dq(E_, c=c, gc=gc):
                                E_.op("dve", lambda e: e.tensor_scalar(
                                    qe2c[:, gc * CH:(gc + 1) * CH],
                                    qeC[:, c * CH:(c + 1) * CH],
                                    egC[:, :], None, ALU.mult),
                                    outs=[QE2C], ins=[QE, EG])
                                E_.op("dve", lambda e: e.tensor_tensor(
                                    GhatT[:, 0:1], GhatT[:, 0:1],
                                    BhC[:, (c + 1) * CH - 1:(c + 1) * CH],
                                    ALU.add), outs=[GH], ins=[GH, BH])
                            sec("vector", _dq)

                        for c in range(8):
                            gc = g * 8 + c
                            cs = slice(c * CH, (c + 1) * CH)
                            par2 = c % 2

                            vt_pe0 = [None]

                            def _vt_pe(E_, c=c, par2=par2):
                                vt_ac0 = ctx.cnt["act"]
                                vt_pe0[0] = ctx.cnt["pe"]
                                for fc in range(16):
                                    bsel = 5 if fc % 2 == 0 else 6
                                    if fc >= 2:
                                        E_.waits(also=[("act",
                                                        vt_ac0 + fc - 1)])
                                    E_.op("pe", lambda e, fc=fc, bsel=bsel:
                                          e.transpose(
                                              pkb(bsel, 0, 64, 0, 128),
                                              xlnC[:, fc * TG + c * CH:
                                                   fc * TG + (c + 1) * CH],
                                              c_idb[:, :]),
                                          outs=[PB[bsel]], ins=[XLC, CONSTS])
                            sec("tensor", _vt_pe)

                            def _vt_ac(E_, c=c, par2=par2):
                                for fc in range(16):
                                    bsel = 5 if fc % 2 == 0 else 6
                                    E_.waits(also=[("pe",
                                                    vt_pe0[0] + fc + 1)])
                                    ev = E_.op(
                                        "act", lambda e, fc=fc, bsel=bsel:
                                        e.copy(vtl[par2][:, fc * 128:
                                                         (fc + 1) * 128],
                                               pkb(bsel, 0, 64, 0, 128)),
                                        outs=[VT[par2]])
                                    PB[bsel].rd.append(ev)
                            sec("scalar", _vt_ac)

                            def _sbf(E_):
                                E_.op("dve", lambda e: e.tensor_copy(
                                    S_bf[:], S_sb[:]), outs=[SBF], ins=[SS])
                            sec("vector", _sbf)

                            def _at(E_, c=c, cs=cs):
                                E_.op("pe", lambda e: e.matmul(
                                    pk(7, 0, 64, 0, 64), keC[:, cs],
                                    qeC[:, cs], start=True, stop=True),
                                    outs=[PB[7]], ins=[KE, QE])
                            sec("tensor", _at)

                            def _am(E_):
                                E_.op("dve", lambda e: e.tensor_tensor(
                                    AsbC[:, :], pk(7, 0, 64, 0, 64),
                                    c_triu[:, :], ALU.mult),
                                    outs=[ASB], ins=[PB[7], CONSTS])
                            sec("vector", _am)

                            def _kt(E_, cs=cs):
                                E_.op("pe", lambda e: e.transpose(
                                    pkb(7, 0, 64, 0, 128), kdC[:, cs],
                                    c_idb[:, :]),
                                    outs=[PB[7]], ins=[KD, CONSTS])
                            sec("tensor", _kt)

                            def _ke(E_):
                                E_.op("act", lambda e: e.copy(
                                    kdT[:, :], pkb(7, 0, 64, 0, 128)),
                                    outs=[KDT], ins=[PB[7]])
                            sec("scalar", _ke)

                            def _omm(E_, c=c, cs=cs, par2=par2):
                                def f(e):
                                    for dvs in range(4):
                                        e.matmul(
                                            pk(dvs, 0, 64), AsbC[:, :],
                                            vtl[par2][:, dvs * TG:
                                                      (dvs + 1) * TG],
                                            start=True, stop=False)
                                        i2 = e.matmul(
                                            pk(dvs, 0, 64), qeC[:, cs],
                                            S_bf[:, dvs * TG:(dvs + 1) * TG],
                                            start=False, stop=True)
                                    return i2
                                E_.op("pe", f,
                                      outs=[PB[0], PB[1], PB[2], PB[3]],
                                      ins=[ASB, VT[par2], QE, SBF])

                                def f2(e):
                                    for dvs in range(4):
                                        i2 = e.matmul(
                                            pk(4 + dvs), kdT[:, :],
                                            vtl[par2][:, dvs * TG:
                                                      (dvs + 1) * TG],
                                            start=True, stop=True)
                                    return i2
                                E_.op("pe", f2,
                                      outs=[PB[4], PB[5], PB[6], PB[7]],
                                      ins=[KDT, VT[par2]])
                            sec("tensor", _omm)

                            def _su(E_, c=c):
                                E_.op("dve", lambda e: e.scalar_tensor_tensor(
                                    S_sb[:], S_sb[:], decC[:, c:c + 1],
                                    pfull[:, 4 * 512:8 * 512], ALU.mult,
                                    ALU.add),
                                    outs=[SS],
                                    ins=[SS, DEC, PB[4], PB[5], PB[6], PB[7]])
                            sec("vector", _su)

                            for piece in range(2):
                                def _oe(E_, gc=gc, piece=piece):
                                    E_.op("act", lambda e: e.activation(
                                        och[:, :],
                                        pfull[0:64, piece * 1024:
                                              piece * 1024 + 1024],
                                        AF.Copy,
                                        scale=rzcA[0:64, gc:gc + 1]),
                                        outs=[OCH],
                                        ins=[PB[2 * piece], PB[2 * piece + 1],
                                             RZC])
                                sec("scalar", _oe)

                                def _os(E_, gc=gc, piece=piece):
                                    E_.op("dos", lambda e: e.dma_start(
                                        out=o_scr[gc, :, piece * 1024:
                                                  piece * 1024 + 1024],
                                        in_=och[:, :]),
                                        outs=[OSCR[gc]], ins=[OCH])
                                sec("scalar", _os)
                        ctx.close_scope([WQK, KS, SF1, SF2, BH, QE, KE, KD,
                                         EQ, DEC, EG, VT[0], VT[1], ASB, KDT,
                                         OCH, ZR])
                    _st.close()
                ctx.close_scope([XLC])

            def _sc(E_):
                ev1 = E_.op("dcc", lambda e: e.dma_start(
                    out=ccs_in[:, :], in_=S_sb[:]), ins=[SS])
                ev2 = E_.op("cc", lambda e: e.collective_compute(
                    "AllGather", ALU.bypass,
                    replica_groups=[[0, 1], [2, 3], [4, 5], [6, 7]],
                    ins=[ccs_in[:, :].opt()], outs=[ccs_out[:, :].opt()]),
                    cc=True, also=[ev1])
                E_.op("dcc", lambda e: e.dma_start(out=S_sb[:],
                                                     in_=ccs_out[0:128, :]),
                      outs=[SS], also=[ev2])
            sec("gpsimd", _sc)

            def _sm(E_):
                E_.op("dve", lambda e: e.tensor_scalar(
                    S_bf[:], S_sb[:], c_pmask[:, :], None, ALU.mult),
                    outs=[SBF], ins=[SS, CONSTS])
            sec("vector", _sm)
            ctx.close_scope([SS])
            _stS.close()

            _st3 = ExitStack()
            _c3 = lambda nm, sh, dt: _st3.enter_context(
                nc.sbuf_tensor(nm, sh, dt))
            och3 = _c3("och3", [64, D], BF16)
            ofm = _c3("ofm", [128, NCH * TG], BF16)
            sfin = _c3("sfin", [128, NCH * TG], BF16)
            goxl = [_c3(f"goxl{i}", [128, TG], BF16) for i in range(2)]
            wsl3 = [_c3(f"wsl3{i}", [128, D], BF16) for i in range(2)]
            sqf3 = _c3("sqf3", [128, TG], BF16)
            if True:
                OCH3, OFM, SFIN = Buf(), Buf(), Buf()
                GOXL = [Buf(), Buf()]
                WSL3 = [Buf(), Buf()]
                SQF3 = Buf()
                crow = rows2[0:1, TG:2 * TG]
                CROW = Buf()
                for g in range(NG):
                    for c in range(8):
                        gc = g * 8 + c

                        def _lo(E_, gc=gc):
                            E_.op("doc", lambda e: e.dma_start(
                                out=och3[:, :], in_=o_scr[gc, :, :]),
                                outs=[OCH3], ins=[OSCR[gc]])
                        sec("sync", _lo)

                        def _dc(E_, gc=gc):
                            def f(e):
                                for dvs in range(4):
                                    i2 = e.matmul(
                                        pk(dvs, 0, 64),
                                        qe2c[:, gc * CH:(gc + 1) * CH],
                                        S_bf[:, dvs * TG:(dvs + 1) * TG],
                                        start=True, stop=True)
                                return i2
                            E_.op("pe", f, outs=[PB[0], PB[1], PB[2], PB[3]],
                                  ins=[QE2C, SBF])
                        sec("tensor", _dc)

                        def _da(E_, gc=gc):
                            E_.op("dve", lambda e: e.scalar_tensor_tensor(
                                och3[:, :], pfull[0:64, 0:2048],
                                rzcA[0:64, gc:gc + 1], och3[:, :],
                                ALU.mult, ALU.add),
                                outs=[OCH3],
                                ins=[PB[0], PB[1], PB[2], PB[3], RZC, OCH3])
                        sec("vector", _da)

                        tr_pe0 = [None]

                        def _tr(E_, c=c):
                            tr_ac0 = ctx.cnt["act"]
                            tr_pe0[0] = ctx.cnt["pe"]
                            for fc in range(16):
                                bsel = 4 + (fc % 2)
                                if fc >= 2:
                                    E_.waits(also=[("act", tr_ac0 + fc - 1)])
                                E_.op("pe", lambda e, fc=fc, bsel=bsel:
                                      e.transpose(
                                          pkb(bsel, 0, 128, 0, 64),
                                          och3[:, fc * 128:(fc + 1) * 128],
                                          c_idb[0:64, 0:64]),
                                      outs=[PB[bsel]], ins=[OCH3, CONSTS])
                        sec("tensor", _tr)

                        def _te(E_, c=c):
                            for fc in range(16):
                                bsel = 4 + (fc % 2)
                                E_.waits(also=[("pe", tr_pe0[0] + fc + 1)])
                                ev = E_.op(
                                    "act", lambda e, fc=fc, bsel=bsel:
                                    e.copy(ofm[:, fc * TG + c * CH:
                                               fc * TG + (c + 1) * CH],
                                           pkb(bsel, 0, 128, 0, 64)),
                                    outs=[OFM])
                                PB[bsel].rd.append(ev)
                        sec("scalar", _te)

                    for fc in range(NCH):
                        def _s1(E_, fc=fc):
                            E_.op("act", lambda e: e.activation(
                                sqf3[:], ofm[:, fc * TG:(fc + 1) * TG],
                                AF.Square), outs=[SQF3], ins=[OFM])
                        sec("scalar", _s1)

                        def _s2(E_, fc=fc):
                            E_.op("pe", lambda e: e.matmul(
                                pk(6, 0, 1), c_onesc[:, :], sqf3[:],
                                start=(fc == 0), stop=(fc == NCH - 1)),
                                outs=[PB[6]], ins=[SQF3, CONSTS])
                        sec("tensor", _s2)

                    def _cr(E_):
                        E_.op("act", lambda e: e.activation(
                            crow, pk(6, 0, 1), AF.Sqrt, scale=1.0 / D,
                            bias=c_cb[0:1, 0:1]),
                            outs=[CROW], ins=[PB[6], CONSTS])
                    sec("scalar", _cr)

                    def _cr2(E_):
                        E_.op("dve", lambda e: e.reciprocal(crow, crow),
                              outs=[CROW], ins=[CROW])
                    sec("vector", _cr2)

                    def _cb(E_):
                        E_.op("pe", lambda e: e.matmul(
                            pk(7), c_onesr[:, :], crow, start=True,
                            stop=True), outs=[PB[7]], ins=[CROW, CONSTS])
                    sec("tensor", _cb)

                    for o in range(NCH):
                        par = o % 2

                        def _lw(E_, g=g, o=o, par=par):
                            E_.op(f"dw3{par}", lambda e: e.dma_start(
                                out=wsl3[par][:, :], in_=wog_b[o, :, :]),
                                outs=[WSL3[par]], also=cast_ev("wog"))
                            E_.op(f"dg3{par}", lambda e: e.dma_start(
                                out=goxl[par][:, :],
                                in_=gox_scr[g, :, o * TG:(o + 1) * TG]),
                                outs=[GOXL[par]], ins=[GOXS[g]])
                        sec("sync", _lw)

                        def _pg(E_, o=o, par=par):
                            def f(e):
                                for fc in range(NCH):
                                    e.matmul(pk(par),
                                             wsl3[par][:, fc * 128:
                                                       (fc + 1) * 128],
                                             ofm[:, fc * TG:(fc + 1) * TG],
                                             start=(fc == 0), stop=False)
                                return e.matmul(pk(par), c_idb[:, :],
                                                goxl[par][:, :],
                                                start=False, stop=True)
                            E_.op("pe", f, outs=[PB[par]],
                                  ins=[WSL3[par], OFM, GOXL[par], CONSTS])
                        sec("tensor", _pg)

                        def _sg(E_, o=o, par=par):
                            E_.op("act", lambda e: e.activation(
                                sfin[:, o * TG:(o + 1) * TG], pk(par),
                                AF.Silu), outs=[SFIN], ins=[PB[par]])
                        sec("scalar", _sg)

                        def _fin(E_, o=o):
                            gn_col = vecs[:, GNW0 + o:GNW0 + o + 1]
                            sl = slice(o * TG, (o + 1) * TG)
                            E_.op("dve", lambda e: e.scalar_tensor_tensor(
                                sfin[:, sl], ofm[:, sl], gn_col,
                                sfin[:, sl], ALU.mult, ALU.mult),
                                outs=[SFIN], ins=[OFM, VECS, SFIN])
                            E_.op("dve", lambda e: e.tensor_tensor(
                                sfin[:, sl], sfin[:, sl], pk(7), ALU.mult),
                                outs=[SFIN], ins=[SFIN, PB[7]])
                        sec("vector", _fin)

                    for o in range(NCH):
                        par = o % 2

                        def _lw(E_, o=o, par=par):
                            E_.op(f"dw3{par}", lambda e: e.dma_start(
                                out=wsl3[par][:, :], in_=wob_b[o, :, :]),
                                outs=[WSL3[par]], also=cast_ev("wob"))
                        sec("sync", _lw)

                        def _pw(E_, o=o, par=par):
                            def f(e):
                                for fc in range(NCH):
                                    i2 = e.matmul(
                                        pk(2 + par),
                                        wsl3[par][:, fc * 128:(fc + 1) * 128],
                                        sfin[:, fc * TG:(fc + 1) * TG],
                                        start=(fc == 0), stop=(fc == NCH - 1))
                                return i2
                            E_.op("pe", f, outs=[PB[2 + par]],
                                  ins=[WSL3[par], SFIN])
                        sec("tensor", _pw)

                        def _rw(E_, g=g, o=o, par=par):
                            xs = x_res[:, o * T + g * TG:o * T + (g + 1) * TG]
                            E_.op("dve", lambda e: e.tensor_tensor(
                                xs, xs, pk(2 + par), ALU.add),
                                outs=[XRESG[g]], ins=[PB[2 + par], XRESG[g]])
                        sec("vector", _rw)
                ctx.close_scope([OCH3, OFM, SFIN, GOXL[0], GOXL[1], WSL3[0],
                                 WSL3[1], SQF3, CROW])
            _st3.close()
            ctx.close_scope([SS, SBF, QE2C, RZC, GH])

        if DEBUG_TAP == 3:
            def _tp(E_):
                for c in range(NCH):
                    E_.op("dbgs", lambda e, c=c: e.dma_start(
                        out=tap[c * 128:(c + 1) * 128, :],
                        in_=x_res[:, c * T:(c + 1) * T]), ins=XRESG)
            sec("gpsimd", _tp)

        emit_ffn(1)

        if DEBUG_DUMP:
            def _dbgw(E_):
                for i in range(16):
                    E_.op("dbgs", lambda e, i=i: e.dma_start(
                        out=dbgw[i, :, :], in_=aWq_b[i, :, :]))
            sec("gpsimd", _dbgw)

        def _out(E_):
            for c in range(NCH):
                E_.op("dout", lambda e, c=c: e.dma_start(
                    out=y[c * 128:(c + 1) * 128, :],
                    in_=x_res[:, c * T:(c + 1) * T]), ins=XRESG)
            E_.waits(also=[("dout", ctx.cnt["dout"]),
                           ("dbgs", ctx.cnt["dbgs"]) if ctx.cnt["dbgs"]
                           else None])
        sec("gpsimd", _out)

    es.close()
    return nc


def _consts():
    n = np.arange(128)
    m = np.arange(64)
    ang = 2 * np.pi * np.outer(n, m) / 128.0
    DRE = (np.cos(ang) * np.sqrt(2.0))
    DRE[:, 0] = 1.0
    DIM = (-np.sin(ang) * np.sqrt(2.0))
    DIM[:, 0] = (-1.0) ** n
    ang2 = 2 * np.pi * np.outer(m, n) / 128.0
    IRE = np.cos(ang2) / 128.0
    IRE[0, :] = 1.0 / 128.0
    IIM = -np.sin(ang2) / 128.0
    IIM[0, :] = ((-1.0) ** n) / 128.0
    ident = np.eye(128)
    triu = np.triu(np.ones((64, 64)))
    return (DRE.astype(np.float32), DIM.astype(np.float32),
            IRE.astype(np.float32), IIM.astype(np.float32),
            ident.astype(np.float32), triu.astype(np.float32))


def _slab_oc(W):
    O, K = W.shape
    no, nk = O // 128, K // 128
    A = W.reshape(no, 128, nk, 128)
    return np.ascontiguousarray(A.transpose(0, 3, 2, 1)).reshape(no, 128, K)


def kernel(L0_attn_a_x, L0_attn_a_W_q, L0_attn_a_W_kv, L0_attn_a_W_o,
           L0_attn_b_Wq, L0_attn_b_Wk, L0_attn_b_Wog, L0_attn_b_Wig,
           L0_attn_b_Wo, L0_attn_b_g_norm_w, L0_ffn_gate_w, L0_ffn_up_w,
           L0_ffn_down_w, L0_ffn_gate_s, L0_ffn_up_s, L0_ffn_down_s):
    x = np.asarray(L0_attn_a_x, dtype=np.float32)
    DRE, DIM, IRE, IIM, ident, triu = _consts()

    aWq_s = _slab_oc(np.asarray(L0_attn_a_W_q, np.float32))
    aWkv_s = _slab_oc(np.asarray(L0_attn_a_W_kv, np.float32))
    aWo_s = _slab_oc(np.asarray(L0_attn_a_W_o, np.float32))
    gw_s = _slab_oc(np.asarray(L0_ffn_gate_w, np.float32))
    uw_s = _slab_oc(np.asarray(L0_ffn_up_w, np.float32))
    dwT = np.ascontiguousarray(np.asarray(L0_ffn_down_w, np.float32).T)
    dw_s = np.ascontiguousarray(dwT.reshape(NI, 128, D))
    bWq_s = _slab_oc(np.asarray(L0_attn_b_Wq, np.float32))[0]
    bWk_s = _slab_oc(np.asarray(L0_attn_b_Wk, np.float32))[0]
    wog_s = _slab_oc(np.asarray(L0_attn_b_Wog, np.float32))
    wig_s = _slab_oc(np.asarray(L0_attn_b_Wig, np.float32))
    wob_s = _slab_oc(np.asarray(L0_attn_b_Wo, np.float32))
    gnw_s = np.ascontiguousarray(
        np.asarray(L0_attn_b_g_norm_w, np.float32).reshape(NCH, 128).T)
    gs_s = np.ascontiguousarray(
        np.asarray(L0_ffn_gate_s, np.float32).reshape(NI, 128).T)
    us_s = np.ascontiguousarray(
        np.asarray(L0_ffn_up_s, np.float32).reshape(NI, 128).T)
    ds_s = np.ascontiguousarray(
        np.asarray(L0_ffn_down_s, np.float32).reshape(NCH, 128).T)
    onesr = np.ones((1, 128), np.float32)
    onesc = np.ones((128, 1), np.float32)
    cbias = np.zeros((128, 2), np.float32)
    cbias[:, 0] = EPS
    cbias[:, 1] = 1.0

    nc = build_nc()
    in_maps = []
    for core in range(8):
        b, p = core // 2, core % 2
        xT = np.ascontiguousarray(x[b, p * T:(p + 1) * T, :].T)
        pmask = np.full((128, 1), float(p), np.float32)
        in_maps.append({
            "xT": xT, "dre": DRE, "dim": DIM, "ire": IRE, "iim": IIM,
            "ident": ident, "triu64": triu, "onesr": onesr, "onesc": onesc,
            "pmask": pmask, "cbias": cbias, "aWq": aWq_s, "aWkv": aWkv_s, "aWo": aWo_s,
            "gw": gw_s, "uw": uw_s, "dw": dw_s, "bWq": bWq_s, "bWk": bWk_s,
            "wog": wog_s, "wig": wig_s, "wob": wob_s, "gnw": gnw_s,
            "gsv": gs_s, "usv": us_s, "dsv": ds_s,
        })
    res = run_bass_kernel_spmd(nc, in_maps, core_ids=list(range(8)))
    out = np.empty((4, 4096, D), np.float32)
    for core in range(8):
        b, p = core // 2, core % 2
        out[b, p * T:(p + 1) * T, :] = res.results[core]["y"].T
    if DEBUG_TAP:
        kernel.taps = [res.results[c]["tap"].T for c in range(8)]
    if DEBUG_DUMP:
        kernel.dbg = [res.results[c]["dbg"] for c in range(8)]
        kernel.dbgw = [res.results[c]["dbgw"] for c in range(8)]
    return out
